# version 04
# baseline (speedup 1.0000x reference)
import os
import numpy as np
import concourse.bass as bass
import concourse.mybir as mybir
from concourse.bass_utils import run_bass_kernel_spmd

F32 = mybir.dt.float32
F32R = mybir.dt.float32r
BF16 = mybir.dt.bfloat16
FP8 = mybir.dt.float8e4
I32 = mybir.dt.int32
AF = mybir.ActivationFunctionType
ALU = mybir.AluOpType
AX = mybir.AxisListType
ts = bass.ts

B, L, D, H = 4, 8192, 1024, 4
DV = 256
OWN = 4096
C = 64
NCH = L // C
GRP = 512
NG = L // GRP
NOG = OWN // GRP
XHALO = 128
MAGIC_HI = float(0x5F3759DF >> 9)
NH = 4

DEBUG = bool(int(os.environ.get("KBG_DEBUG", "0")))
NG_LIM = int(os.environ.get("KBG_NG", str(NG)))
NOG_LIM = int(os.environ.get("KBG_NOG", str(NOG)))


def build_program():
    nc = bass.Bass("TRN2", target_bir_lowering=False)

    d = {}
    def din(name, shape, dt=F32):
        d[name] = nc.dram_tensor(name, shape, dt, kind="ExternalInput").ap()
        return d[name]

    x_seq = din("x_seq", [L, D])
    x_own = din("x_own", [OWN + XHALO, D])
    wqT = din("wqT", [D, 512]); wkT = din("wkT", [D, 512]); wbT = din("wbT", [D, 2])
    wvT = din("wvT", [D, D]); wvsT = din("wvsT", [D, 512])
    w1T = din("w1T", [1056, 2048]); w2T = din("w2T", [2048, 16])
    woT = din("woT", [D, D]); wfT = din("wfT", [D, D])
    qcw = din("qcw", [128, 16]); kcw = din("kcw", [128, 16])
    vcw = din("vcw", [128, 32]); vscw = din("vscw", [128, 16])
    firs = din("firs", [128, 24]); firl = din("firl", [128, 504])
    gb1 = din("gb1", [128, 16]); bb = din("bb", [16, 1])
    ltemp = din("ltemp", [1, 4]); fraw = din("fraw", [1, 16])
    onw4 = din("onw4", [128, 8]); fbias = din("fbias", [D])
    nonce = din("nonce", [1, 2], I32); meid = din("meid", [1, 1], I32)

    out_d = nc.dram_tensor("out", [OWN, D], F32, kind="ExternalOutput").ap()
    dbg = {}
    if DEBUG:
        for nm, shp in [("q_fm", [512, GRP]), ("o0", [C, 512]), ("x1", [128, D]),
                        ("probs", [128, 64]), ("pths", [128, 4 * D]), ("amax", [128, 4]),
                        ("kn0", [C, 512]), ("uw0", [C, 1024]), ("stat", [128, 128]),
                        ("xfm", [128, 8 * GRP]), ("qpre", [128, GRP + 4])]:
            dbg[nm] = nc.dram_tensor("dbg_" + nm, shp, F32, kind="ExternalOutput").ap()

    sh_o = nc.dram_tensor("sh_o", [2 * L, 512], BF16, kind="Internal", addr_space="Shared").ap()
    sh_flag = nc.dram_tensor("sh_flag", [8, 16], I32, kind="Internal", addr_space="Shared").ap()
    sh_sc = nc.dram_tensor("sh_sc", [4, 16], F32, kind="Internal", addr_space="Shared").ap()
    x1_spill = nc.dram_tensor("x1_spill", [OWN, D], F32, kind="Internal").ap()
    o_loc = nc.dram_tensor("o_loc", [L, 512], BF16, kind="Internal").ap()
    o_stage = nc.dram_tensor("o_stage", [2 * OWN, 512], BF16, kind="Internal").ap()
    loc_sc = nc.dram_tensor("loc_sc", [1, 32], F32, kind="Internal").ap()

    from contextlib import ExitStack
    with ExitStack() as ctx:
        def sb(name, shape, dt):
            return ctx.enter_context(nc.sbuf_tensor(name, shape, dt))
        def pst(name, shape, dt=F32):
            return ctx.enter_context(nc.psum_tensor(name, shape, dt))
        def sem(name):
            return ctx.enter_context(nc.semaphore(name))

        wbig_h = sb("wbig", [128, 9 * 2048], BF16)
        wbig = wbig_h[:, :]
        wq_sb = wbig[:, 0:4096]
        wk_sb = wbig[:, 4096:8192]
        wvs_sb = wbig[:, 8192:12288]
        w1_sb = wbig[:, 0:18432]
        wb_sb = sb("wb_sb", [128, 8 * 2], BF16)
        wv_sb = sb("wv_sb", [128, 8 * 1024], BF16)
        w2_sb = sb("w2_sb", [128, 16 * 16], BF16)
        wo_sb = sb("wo_sb", [128, 8 * 1024], BF16)

        qkdiag_h = sb("qkdiag", [128, 32 * 128], BF16)
        qdiag = qkdiag_h[:, 0:16 * 128]
        kdiag = qkdiag_h[:, 16 * 128:32 * 128]
        vdiag = qkdiag_h[:, 0:32 * 128]
        sdiag_h = sb("sdiag", [128, 24 * 128], BF16)
        sdiag = sdiag_h[:, :]
        vsdiag = sdiag_h[:, 0:16 * 128]
        firl_sb = sb("firl_sb", [128, 8 * 63], F32)
        firs_sb = sb("firs_sb", [128, 8 * 3], F32)
        cw_sb = sb("cw_sb", [128, 64], F32)
        cw2_sb = sb("cw2_sb", [128, 16], F32)

        ident_b = sb("ident_b", [128, 128], BF16)
        ident_f = sb("ident_f", [128, 128], F32)
        ones_b = sb("ones_b", [128, 128], BF16)
        mask_low = sb("mask_low", [C, C], BF16)
        mask_upc = sb("mask_upc", [C, C], F32)
        gb1_sb = sb("gb1_sb", [128, 16], F32)
        bb_sb = sb("bb_sb", [16, 1], F32)
        itemp_sb = sb("itemp_sb", [128, 64], F32)
        floor_sb = sb("floor_sb", [128, 64], F32)
        fbias_sb = sb("fbias_sb", [128, 1024], BF16)
        onw_sb = sb("onw_sb", [128, 8], F32)
        ws2_bc = sb("ws2_bc", [128, 1], F32)
        sc_bc = sb("sc_bc", [128, 2], F32)
        sA = sb("sA", [128, 128], F32)
        sB = sb("sB", [128, 128], F32)
        sC = sb("sC", [128, 128], F32)
        nonce_sb = sb("nonce_sb", [1, 2], I32)
        me_sb = sb("me_sb", [1, 1], I32)
        flag_sb = sb("flag_sb", [1, 16], I32)

        xt_f = sb("xt_f", [128, D], F32)
        xt_b = sb("xt_b", [128, D], BF16)
        xfm = sb("xfm", [128, 8 * GRP], BF16)
        sbig1_h = sb("sbig1", [128, 8 * (GRP + NH)], BF16)
        wf8_sb = sbig1_h[:, 0:4096].bitcast(FP8)
        qpre = sbig1_h[:, 0:4 * (GRP + NH)]
        kpre = sbig1_h[:, 4 * (GRP + NH):8 * (GRP + NH)]
        vop = sbig1_h[:, 0:8 * (GRP + NH)]
        sbig2_h = sb("sbig2", [128, 8 * (GRP + 64)], BF16)
        vspre = sbig2_h[:, 0:4 * (GRP + NH)]
        qfm = sbig2_h[:, 4 * (GRP + NH):4 * (GRP + NH) + 4 * GRP]
        vow = sbig2_h[:, 0:8 * (GRP + 64)]
        sbig3_h = sb("sbig3", [128, 8 * GRP], BF16)
        kfm = sbig3_h[:, 0:4 * GRP]
        vsfm = sbig3_h[:, 4 * GRP:8 * GRP]
        pth_s_alias = sbig3_h[:, 0:4 * D]
        tstage = sb("tstage", [128, 2 * GRP], F32)
        beta_g = sb("beta_g", [2, GRP], F32)

        q_r = sb("q_r", [128, 4 * C], F32R)
        qsq = sb("qsq", [128, 4 * C], BF16)
        kn_tm = sb("kn_tm", [C, 2 * DV], F32R)
        kn_fm = sb("kn_fm", [128, 4 * C], BF16)
        u_tm = sb("u_tm", [C, 2 * DV], F32R)
        w_fm = sb("w_fm", [128, 4 * C], F32R)
        ui_tm = sb("ui_tm", [C, 2 * DV], F32R)
        attnT = sb("attnT", [C, 2 * C], F32R)
        Lt = sb("Lt", [C, 2 * C], BF16)
        LtT = sb("LtT", [C, 2 * C], BF16)
        Xp = sb("Xp", [C, 2 * C], BF16)
        XpT = sb("XpT", [C, 2 * C], BF16)
        PT = sb("PT", [C, 2 * C], BF16)
        PT2 = sb("PT2", [C, 2 * C], BF16)
        PTr = sb("PTr", [C, 2 * C], F32R)
        beta_tm = sb("beta_tm", [C, 4], F32)
        rs_sc = sb("rs_sc", [128, 16], F32)
        rs_a = sb("rs_a", [128, 16], F32)
        rs_b = sb("rs_b", [128, 16], F32)
        rs_i = sb("rs_i", [128, 16], I32)
        o_tile = sb("o_tile", [C, 512], BF16)

        fir_a = sb("fir_a", [128, GRP], F32)
        logit_f = fir_a[0:16, 0:GRP]
        fir_b = sb("fir_b", [128, GRP], F32)
        fir_o = sb("fir_o", [128, GRP], BF16)
        pth_v = sb("pth_v", [128, 4 * D], BF16)
        pth_s = pth_s_alias
        pth_l = sb("pth_l", [128, 4 * D], BF16)
        pth_d = sb("pth_d", [128, 4 * D], BF16)
        bvbk = sb("bvbk", [C, 2 * 512], F32R)
        S_sb = sb("S_sb", [128, 2 * 512], F32R)
        wstage = pth_d[:, 0:4096].bitcast(F32)
        sq_t = tstage[:, 0:D]
        stat_tm = sb("stat_tm", [128, 4 * 32], F32)
        stat_fm = sb("stat_fm", [32, 4 * 128], BF16)
        hmid = sb("hmid", [128, 2 * GRP], BF16)
        logit_t = sb("logit_t", [128, 64], F32)
        prob_t = sb("prob_t", [128, 64], F32)
        sm_a = sb("sm_a", [128, 64], F32)
        sm_b = sb("sm_b", [128, 96], F32)
        firb2 = None
        comb_a = sb("comb_a", [128, D], BF16)
        comb_b = sb("comb_b", [128, D], BF16)
        orms = sb("orms", [128, D], BF16)
        ofm = sb("ofm", [128, 8 * 128], BF16)
        y_t = sb("y_t", [128, D], F32)
        x1_t = sb("x1_t", [128, D], F32)
        k_tm = x1_t[0:C, 0:2 * DV]
        v_tm = x1_t[0:C, 2 * DV:4 * DV]
        ksq = y_t[0:C, 0:2 * DV]
        ln_sc = sb("ln_sc", [128, 16], F32)
        amax_run = sb("amax_run", [128, 4], F32)
        xs_b = orms
        xs_fmb = ofm
        xs_fm8 = sb("xs_fm8", [128, 8 * 128], FP8)
        ffn_t = comb_a
        out_t = x1_t

        pA = pst("pA", [128, 512])
        pB = pst("pB", [128, 512])
        pC_ = pst("pC", [128, 512])
        pD_ = pst("pD", [128, 512])
        pS0 = pst("pS0", [128, 512])
        pS1 = pst("pS1", [128, 512])
        pT_b = pst("pT_b", [128, 128], BF16)
        pT_f = pst("pT_f", [128, 128], F32)

        s_ld = sem("s_ld")
        s_ld2 = sem("s_ld2")
        s_t = sem("s_t")
        s_v = sem("s_v")
        s_a = sem("s_a")
        s_g = sem("s_g")
        s_st = sem("s_st")
        s_p = sem("s_p")
        s_gd = sem("s_gd")
        s_dbg = sem("s_dbg")

        ctx.enter_context(nc.allow_non_contiguous_dma(reason="tiny strided preamble loads"))
        block = ctx.enter_context(nc.Block())

        state = {"cur": None, "eng": None, "cnt": {}, "rv": {}}

        def I(name, fn, *incs):
            for s_, amt in incs:
                state["cnt"][s_.name] = state["cnt"].get(s_.name, 0) + amt
            if state["cur"] == name:
                ins = fn(state["eng"])
                for s_, amt in incs:
                    ins.then_inc(s_, amt)

        def W(name, s_):
            v = state["cnt"].get(s_.name, 0)
            if state["cur"] == name and v > 0:
                state["eng"].wait_ge(s_, v)

        def DR(name):
            if state["cur"] == name:
                state["eng"].drain()

        def emit(pass_name, eng):
            state["cur"], state["eng"], state["cnt"] = pass_name, eng, {}
            sy = "sync"; te = "tensor"; ve = "vector"; sc = "scalar"; gp = "gpsimd"

            def ld(dst, src, s_=s_ld, q=sy):
                I(q, lambda e: e.dma_start(out=dst, in_=src), (s_, 16))

            def ld2(dst, src):
                I(sc, lambda e: e.dma_start(out=dst, in_=src), (s_ld2, 16))

            def dmat(dst, src):
                I(sc, lambda e: e.dma_start_transpose(out=dst, in_=src), (s_ld2, 16))

            def vop_(fn, *incs):
                I(ve, fn, *(incs or ((s_v, 1),)))

            ld(nonce_sb[:], nonce[:])
            ld(me_sb[:], meid[:])

            I(gp, lambda e: e.memset(ident_b[:], 0.0))
            I(gp, lambda e: e.affine_select(out=ident_b[:], in_=ident_b[:],
                                            compare_op=ALU.not_equal, fill=1.0, base=0,
                                            pattern=[[-1, 128]], channel_multiplier=1))
            I(gp, lambda e: e.memset(ident_f[:], 0.0))
            I(gp, lambda e: e.affine_select(out=ident_f[:], in_=ident_f[:],
                                            compare_op=ALU.not_equal, fill=1.0, base=0,
                                            pattern=[[-1, 128]], channel_multiplier=1))
            I(gp, lambda e: e.memset(ones_b[:], 1.0))
            I(gp, lambda e: e.memset(mask_low[:], 1.0))
            I(gp, lambda e: e.affine_select(out=mask_low[:], in_=mask_low[:],
                                            compare_op=ALU.is_gt, fill=0.0, base=0,
                                            pattern=[[-1, C]], channel_multiplier=1))
            I(gp, lambda e: e.memset(mask_upc[:], 1.0))
            I(gp, lambda e: e.affine_select(out=mask_upc[:], in_=mask_upc[:],
                                            compare_op=ALU.is_ge, fill=0.0, base=0,
                                            pattern=[[1, C]], channel_multiplier=-1), (s_g, 1))

            def load_w(dst, src, nkk, ncols, kk0=0):
                for kk in range(nkk):
                    ld(wstage[:, 0:ncols], src[kk * 128:(kk + 1) * 128, :])
                    W(ve, s_ld)
                    vop_(lambda e, kk=kk: e.tensor_copy(
                        dst[:, (kk0 + kk) * ncols:(kk0 + kk + 1) * ncols], wstage[:, 0:ncols]))
                    W(sy, s_v)

            load_w(wq_sb, wqT, 8, 512)
            load_w(wk_sb, wkT, 8, 512)
            load_w(wb_sb, wbT, 8, 2)
            load_w(wv_sb, wvT, 8, 1024)
            load_w(wvs_sb, wvsT, 8, 512)
            load_w(w2_sb, w2T, 16, 16)

            ld(onw_sb[:, 0:8], onw4[:])
            for kk in range(8):
                ld(wstage[:, 0:1024], woT[kk * 128:(kk + 1) * 128, :])
                W(ve, s_ld)
                vop_(lambda e, kk=kk: e.tensor_scalar(
                    wo_sb[:, kk * 1024:(kk + 1) * 1024], wstage[:, 0:1024],
                    onw_sb[:, kk:kk + 1], None, ALU.mult))
                W(sy, s_v)

            for kk in range(8):
                ld(wstage[:, 0:1024], wfT[kk * 128:(kk + 1) * 128, :])
                W(ve, s_ld)
                if kk == 0:
                    vop_(lambda e: e.tensor_reduce(sA[:, 0:1], wstage[:, 0:1024], AX.X,
                                                   ALU.max, apply_absolute_value=True))
                else:
                    vop_(lambda e: e.tensor_reduce(sA[:, 1:2], wstage[:, 0:1024], AX.X,
                                                   ALU.max, apply_absolute_value=True))
                    DR(ve)
                    vop_(lambda e: e.tensor_tensor(sA[:, 2:3], sA[:, 0:1], sA[:, 1:2], ALU.max))
                    DR(ve)
                    vop_(lambda e: e.tensor_copy(sA[:, 0:1], sA[:, 2:3]))
                    DR(ve)
                W(sy, s_v)
            W(te, s_v)
            I(te, lambda e: e.transpose(pT_f[0:1, 0:128], sA[:, 0:1], ident_f[:]), (s_t, 1))
            W(ve, s_t)
            vop_(lambda e: e.tensor_reduce(sA[0:1, 3:4], pT_f[0:1, 0:128], AX.X, ALU.max))
            DR(ve)
            vop_(lambda e: e.tensor_scalar(sA[0:1, 4:5], sA[0:1, 3:4], 1e-12, None, ALU.max))
            DR(ve)
            vop_(lambda e: e.reciprocal(sA[0:1, 5:6], sA[0:1, 4:5]))
            DR(ve)
            vop_(lambda e: e.tensor_scalar(sA[0:1, 6:7], sA[0:1, 5:6], 448.0, 448.0, ALU.mult, ALU.min))
            DR(ve)
            vop_(lambda e: e.reciprocal(sA[0:1, 7:8], sA[0:1, 6:7]))
            DR(ve)
            vop_(lambda e: e.tensor_scalar(sA[0:1, 8:9], sA[0:1, 6:7], 0.5, None, ALU.mult))
            DR(ve)
            W(sy, s_v)
            ld(loc_sc[0:1, 0:9], sA[0:1, 0:9])
            W(sy, s_ld)
            ld(ws2_bc[:], loc_sc[0:1, 8:9].broadcast_to((128, 1)))

            ld(gb1_sb[:, 0:16], gb1[:])
            ld(bb_sb[:, 0:1], bb[:])
            I("gpsimd", lambda e: e.dma_start(out=fbias_sb[:], in_=fbias[None, :].broadcast_to((128, 1024))), (s_gd, 16))
            ld(sA[0:1, 16:20], ltemp[:])
            ld(sA[0:1, 20:36], fraw[:])
            ld(cw_sb[:, 0:16], qcw[:])
            ld(cw_sb[:, 16:32], kcw[:])
            ld(cw_sb[:, 32:64], vcw[:])
            ld(cw2_sb[:, 0:16], vscw[:])
            ld(firs_sb[:, 0:24], firs[:])
            ld(firl_sb[:, 0:504], firl[:])

            W(sc, s_ld)
            I(sc, lambda e: e.activation(sB[0:1, 0:4], sA[0:1, 16:20], AF.Abs), (s_a, 1))
            DR(sc)
            I(sc, lambda e: e.activation(sB[0:1, 4:8], sB[0:1, 0:4], AF.Exp, scale=-1.0), (s_a, 1))
            DR(sc)
            I(sc, lambda e: e.activation(sB[0:1, 8:12], sB[0:1, 4:8], AF.Ln, bias=1.0), (s_a, 1))
            DR(sc)
            W(ve, s_a)
            vop_(lambda e: e.tensor_scalar(sB[0:1, 12:16], sA[0:1, 16:20], 0.0, None, ALU.max))
            DR(ve)
            vop_(lambda e: e.tensor_tensor(sB[0:1, 16:20], sB[0:1, 8:12], sB[0:1, 12:16], ALU.add))
            DR(ve)
            vop_(lambda e: e.tensor_scalar(sB[0:1, 20:24], sB[0:1, 16:20], 1e-4, None, ALU.add))
            DR(ve)
            vop_(lambda e: e.reciprocal(sB[0:1, 24:28], sB[0:1, 20:24]))
            DR(ve)
            W(sc, s_v)
            I(sc, lambda e: e.activation(sB[0:1, 32:48], sA[0:1, 20:36], AF.Exp, scale=-1.0), (s_a, 1))
            DR(sc)
            W(ve, s_a)
            vop_(lambda e: e.tensor_scalar(sB[0:1, 48:64], sB[0:1, 32:48], 1.0, None, ALU.add))
            DR(ve)
            vop_(lambda e: e.reciprocal(sB[0:1, 64:80], sB[0:1, 48:64]))
            DR(ve)
            vop_(lambda e: e.tensor_scalar(sB[0:1, 80:96], sB[0:1, 64:80], 0.05, None, ALU.mult))
            DR(ve)
            for hh in range(4):
                for pp_ in range(4):
                    vop_(lambda e, hh=hh, pp_=pp_: e.tensor_copy(
                        sB[0:1, 96 + hh * 4 + pp_: 97 + hh * 4 + pp_], sB[0:1, 24 + hh:25 + hh]))
            DR(ve)
            W(sy, s_v)
            ld(loc_sc[0:1, 0:16], sB[0:1, 96:112])
            W(sy, s_ld)
            ld(itemp_sb[:, 0:16], loc_sc[0:1, 0:16].broadcast_to((128, 16)))
            ld(loc_sc[0:1, 16:32], sB[0:1, 80:96])
            W(sy, s_ld)
            ld(floor_sb[:, 0:16], loc_sc[0:1, 16:32].broadcast_to((128, 16)))
            W(ve, s_ld)
            for r in range(1, 4):
                vop_(lambda e, r=r: e.tensor_copy(itemp_sb[:, r * 16:(r + 1) * 16], itemp_sb[:, 0:16]))
                vop_(lambda e, r=r: e.tensor_copy(floor_sb[:, r * 16:(r + 1) * 16], floor_sb[:, 0:16]))

            def build_diags(dst, wsrc, col0, n_rb, n_tap):
                for rb in range(n_rb):
                    for t in range(n_tap):
                        cc0 = (rb * n_tap + t) * 128
                        vop_(lambda e, cc0=cc0, rb=rb, t=t: e.tensor_scalar(
                            dst[:, cc0:cc0 + 128], ones_b[:, 0:128],
                            wsrc[:, col0 + rb * n_tap + t: col0 + rb * n_tap + t + 1],
                            None, ALU.mult))
                W(gp, s_v)
                n = n_rb * n_tap
                I(gp, lambda e: e.affine_select(out=dst[:, 0:n * 128], in_=dst[:, 0:n * 128],
                                                compare_op=ALU.is_equal, fill=0.0, base=0,
                                                pattern=[[0, n], [-1, 128]], channel_multiplier=1),
                  (s_g, 1))

            W(ve, s_ld)
            build_diags(qdiag, cw_sb, 0, 4, 4)
            build_diags(kdiag, cw_sb, 16, 4, 4)
            build_diags(vsdiag, cw2_sb, 0, 4, 4)

            for t_sb in (qpre, kpre, vspre):
                for rb in range(4):
                    vop_(lambda e, t_sb=t_sb, rb=rb: e.memset(
                        t_sb[:, rb * (GRP + NH):rb * (GRP + NH) + NH], 0.0))
            vop_(lambda e: e.memset(amax_run[:], 0.0))

            if state["cur"] == sy:
                W(sy, s_ld)
                state["rv"]["noncer"] = state["eng"].alloc_register("noncer")
                state["eng"].reg_load(state["rv"]["noncer"], nonce_sb[0:1, 0:1])
                state["rv"]["pollr"] = state["eng"].alloc_register("pollr")
            W(sy, s_g)
            W(sy, s_v)
            W(sy, s_a)
            W(te, s_g)

            if state["cur"] == sy:
                me2 = nc.partition_id() % 2
                pp2 = (nc.partition_id() + 1) % 2
                V = state["rv"]
                V["o_wr"] = sh_o[ts(me2, L), :]
                V["o_rd0"] = sh_o[0:L, :][ts(me2, OWN), :]
                V["o_rd1"] = sh_o[L:2 * L, :][ts(me2, OWN), :]
                V["fl_wr"] = sh_flag[ts(me2, 1), :]
                V["fl_rd_all"] = sh_flag[ts(pp2, 1), :]
                V["flam_wr"] = sh_flag[ts(me2 + 2, 1), :]
                V["flam_rd"] = sh_flag[ts(pp2 + 2, 1), :]
                V["sc_wr"] = sh_sc[ts(me2, 1), :]
                V["sc_rd"] = sh_sc[ts(pp2, 1), :]

            for g in range(NG_LIM):
                t0 = g * GRP
                for ti in range(4):
                    W(sy, s_v)
                    ld(xt_f[:], x_seq[t0 + ti * 128: t0 + (ti + 1) * 128, :])
                    W(ve, s_ld)
                    vop_(lambda e: e.tensor_copy(xt_b[:, 0:D], xt_f[:]))
                    W(sc, s_v)
                    W(sc, s_t)
                    for kk in range(8):
                        dmat(xfm[:, kk * GRP + ti * 128: kk * GRP + (ti + 1) * 128],
                             xt_b[:, kk * 128:(kk + 1) * 128])
                    W(ve, s_ld2)
                W(te, s_ld2)

                def proj(pre, w_sb, n_rb, ncols):
                    for rb in range(n_rb):
                        pp = pA if (rb % 2 == 0) else pB
                        for kk in range(8):
                            I(te, lambda e, pp=pp, kk=kk, rb=rb: e.matmul(
                                pp[:, 0:GRP],
                                w_sb[:, kk * ncols + rb * 128: kk * ncols + rb * 128 + 128],
                                xfm[:, kk * GRP:(kk + 1) * GRP],
                                start=(kk == 0), stop=(kk == 7)), (s_t, 1))
                        W(ve, s_t)
                        base = rb * (GRP + NH)
                        if g == 0:
                            vop_(lambda e, pre=pre, base=base: e.memset(
                                pre[:, base:base + NH], 0.0))
                        else:
                            vop_(lambda e, pre=pre, base=base: e.tensor_copy(
                                pre[:, base:base + NH], pre[:, base + GRP:base + GRP + NH]))
                        DR(ve)
                        vop_(lambda e, pre=pre, base=base, pp=pp: e.tensor_copy(
                            pre[:, base + NH:base + NH + GRP], pp[:, 0:GRP]))
                        W(te, s_v)

                proj(qpre, wq_sb, 4, 512)
                proj(kpre, wk_sb, 4, 512)
                proj(vspre, wvs_sb, 4, 512)
                for kk in range(8):
                    I(te, lambda e, kk=kk: e.matmul(
                        pC_[0:2, 0:GRP], wb_sb[:, kk * 2:kk * 2 + 2],
                        xfm[:, kk * GRP:(kk + 1) * GRP],
                        start=(kk == 0), stop=(kk == 7)), (s_t, 1))
                W(sc, s_t)
                I(sc, lambda e: e.activation(tstage[0:2, 0:GRP], pC_[0:2, 0:GRP],
                                             AF.Tanh, scale=0.5), (s_a, 1))
                W(ve, s_a)
                W(ve, s_t)
                vop_(lambda e, t0=t0: e.tensor_scalar(beta_g[0:2, 0:GRP],
                                                      tstage[0:2, 0:GRP], 0.5, 0.5,
                                                      ALU.mult, ALU.add))
                W(te, s_v)

                def conv_silu(pre, dfm, diag, n_rb):
                    for rb in range(n_rb):
                        ppc = pC_ if (rb % 2 == 0) else pD_
                        base = rb * (GRP + NH)
                        for t in range(4):
                            I(te, lambda e, ppc=ppc, diag=diag, rb=rb, t=t, base=base, pre=pre:
                              e.matmul(ppc[:, 0:GRP],
                                       diag[:, (rb * 4 + t) * 128:(rb * 4 + t) * 128 + 128],
                                       pre[:, base + 1 + t: base + 1 + t + GRP],
                                       start=(t == 0), stop=(t == 3)), (s_t, 1))
                        W(sc, s_t)
                        I(sc, lambda e, ppc=ppc: e.activation(tstage[:, 0:GRP], ppc[:, 0:GRP],
                                                              AF.Tanh, scale=0.5), (s_a, 1))
                        W(ve, s_a)
                        vop_(lambda e: e.tensor_scalar(tstage[:, GRP:2 * GRP], tstage[:, 0:GRP],
                                                       0.5, 0.5, ALU.mult, ALU.add))
                        DR(ve)
                        W(ve, s_t)
                        vop_(lambda e, dfm=dfm, rb=rb, ppc=ppc: e.tensor_tensor(
                            dfm[:, rb * GRP:(rb + 1) * GRP], tstage[:, GRP:2 * GRP],
                            ppc[:, 0:GRP], ALU.mult))
                        W(te, s_v)

                conv_silu(qpre, qfm, qdiag, 4)
                conv_silu(kpre, kfm, kdiag, 4)
                conv_silu(vspre, vsfm, vsdiag, 4)

                if DEBUG and g == 0:
                    W(gp, s_v)
                    W(gp, s_a)
                    W(gp, s_ld2)
                    I(gp, lambda e: e.dma_start(out=dbg["xfm"][:], in_=xfm[:, 0:8 * GRP]), (s_dbg, 16))
                    I(gp, lambda e: e.dma_start(out=dbg["qpre"][:], in_=qpre[:, 0:GRP + NH]), (s_dbg, 16))
                    for rb in range(4):
                        I(gp, lambda e, rb=rb: e.dma_start(
                            out=dbg["q_fm"][rb * 128:(rb + 1) * 128, :],
                            in_=qfm[:, rb * GRP:(rb + 1) * GRP]), (s_st, 16))
                    W(sy, s_dbg)
                    W(ve, s_dbg)
                    W(sc, s_dbg)
                    W(te, s_dbg)
                    W(gp, s_dbg)

                for jj in range(GRP // C):
                    j = g * (GRP // C) + jj
                    cc = jj * C
                    W(te, s_v)
                    I(te, lambda e, cc=cc: e.transpose(
                        pT_f[0:C, 0:2], beta_g[0:2, cc:cc + C],
                        ident_f[0:2, 0:2]), (s_t, 1))
                    W(ve, s_t)
                    vop_(lambda e: e.tensor_copy(beta_tm[:, 0:2], pT_f[0:C, 0:2]))
                    for rbk in range(4):
                        W(te, s_v)
                        I(te, lambda e, rbk=rbk, cc=cc: e.transpose(
                            pT_b[0:C, 0:128], kfm[:, rbk * GRP + cc: rbk * GRP + cc + C],
                            ident_b[:]), (s_t, 1))
                        W(ve, s_t)
                        vop_(lambda e, rbk=rbk: e.tensor_copy(
                            k_tm[:, rbk * 128:(rbk + 1) * 128], pT_b[0:C, 0:128]))
                        W(te, s_v)
                        I(te, lambda e, rbk=rbk, cc=cc: e.transpose(
                            pT_b[0:C, 0:128], vsfm[:, rbk * GRP + cc: rbk * GRP + cc + C],
                            ident_b[:]), (s_t, 1))
                        W(ve, s_t)
                        vop_(lambda e, rbk=rbk: e.tensor_copy(
                            v_tm[:, rbk * 128:(rbk + 1) * 128], pT_b[0:C, 0:128]))
                        vop_(lambda e, rbk=rbk, cc=cc: e.tensor_copy(
                            q_r[:, rbk * C:(rbk + 1) * C],
                            qfm[:, rbk * GRP + cc: rbk * GRP + cc + C]))
                        vop_(lambda e, rbk=rbk, cc=cc: e.tensor_tensor(
                            qsq[:, rbk * C:(rbk + 1) * C],
                            qfm[:, rbk * GRP + cc: rbk * GRP + cc + C],
                            qfm[:, rbk * GRP + cc: rbk * GRP + cc + C], ALU.mult))
                    W(te, s_v)
                    for hh in range(2):
                        for sub in range(2):
                            I(te, lambda e, hh=hh, sub=sub: e.matmul(
                                pC_[0:C, 0:1], qsq[:, (hh * 2 + sub) * C:(hh * 2 + sub + 1) * C],
                                ones_b[:, 0:1], start=(sub == 0), stop=(sub == 1)), (s_t, 1))
                        W(ve, s_t)
                        vop_(lambda e, hh=hh: e.tensor_copy(rs_sc[0:C, hh:hh + 1], pC_[0:C, 0:1]))
                        W(te, s_v)
                    vop_(lambda e: e.tensor_tensor(ksq[:, 0:2 * DV], k_tm[:, 0:2 * DV],
                                                   k_tm[:, 0:2 * DV], ALU.mult))
                    DR(ve)
                    vop_(lambda e: e.tensor_reduce(
                        rs_sc[0:C, 2:4], ksq[:, 0:2 * DV].rearrange("p (h dv) -> p h dv", h=2),
                        AX.X, ALU.add))
                    DR(ve)
                    vop_(lambda e: e.tensor_scalar(rs_a[0:C, 0:4], rs_sc[0:C, 0:4], 1e-6, None, ALU.add))
                    DR(ve)
                    vop_(lambda e: e.tensor_scalar(rs_i[0:C, 0:4], rs_a[0:C, 0:4].bitcast(I32),
                                                   10, None, ALU.logical_shift_right))
                    DR(ve)
                    vop_(lambda e: e.tensor_copy(rs_b[0:C, 0:4], rs_i[0:C, 0:4]))
                    DR(ve)
                    vop_(lambda e: e.tensor_scalar(rs_sc[0:C, 4:8], rs_b[0:C, 0:4], MAGIC_HI, -1.0,
                                                   ALU.subtract, ALU.mult))
                    DR(ve)
                    vop_(lambda e: e.tensor_copy(rs_i[0:C, 4:8], rs_sc[0:C, 4:8]))
                    DR(ve)
                    vop_(lambda e: e.tensor_scalar(rs_i[0:C, 8:12], rs_i[0:C, 4:8], 9, None,
                                                   ALU.logical_shift_left))
                    DR(ve)
                    vop_(lambda e: e.tensor_copy(rs_b[0:C, 4:8], rs_i[0:C, 8:12].bitcast(F32)))
                    DR(ve)
                    for it in range(3):
                        vop_(lambda e: e.tensor_tensor(rs_b[0:C, 8:12], rs_b[0:C, 4:8],
                                                       rs_b[0:C, 4:8], ALU.mult))
                        DR(ve)
                        vop_(lambda e: e.tensor_tensor(rs_b[0:C, 12:16], rs_b[0:C, 8:12],
                                                       rs_a[0:C, 0:4], ALU.mult))
                        DR(ve)
                        vop_(lambda e: e.tensor_scalar(rs_sc[0:C, 8:12], rs_b[0:C, 12:16],
                                                       -0.5, 1.5, ALU.mult, ALU.add))
                        DR(ve)
                        vop_(lambda e: e.tensor_tensor(rs_sc[0:C, 12:16], rs_b[0:C, 4:8],
                                                       rs_sc[0:C, 8:12], ALU.mult))
                        DR(ve)
                        vop_(lambda e: e.tensor_copy(rs_b[0:C, 4:8], rs_sc[0:C, 12:16]))
                        DR(ve)
                    for hh in range(2):
                        vop_(lambda e, hh=hh: e.tensor_scalar(
                            kn_tm[:, hh * DV:(hh + 1) * DV], k_tm[:, hh * DV:(hh + 1) * DV],
                            rs_b[0:C, 6 + hh:7 + hh], None, ALU.mult))
                        vop_(lambda e, hh=hh: e.tensor_scalar(
                            bvbk[:, hh * 512:hh * 512 + DV], v_tm[:, hh * DV:(hh + 1) * DV],
                            beta_tm[:, hh:hh + 1], None, ALU.mult))
                    DR(ve)
                    for hh in range(2):
                        vop_(lambda e, hh=hh: e.tensor_scalar(
                            bvbk[:, hh * 512 + DV:hh * 512 + 512],
                            kn_tm[:, hh * DV:(hh + 1) * DV],
                            beta_tm[:, hh:hh + 1], None, ALU.mult))
                    W(te, s_v)
                    for rbk in range(4):
                        I(te, lambda e, rbk=rbk: e.transpose(
                            pT_f[0:128, 0:C],
                            kn_tm[:, rbk * 128:(rbk + 1) * 128].bitcast(F32),
                            ident_f[0:C, 0:C]), (s_t, 1))
                        W(ve, s_t)
                        vop_(lambda e, rbk=rbk: e.tensor_copy(
                            kn_fm[:, rbk * C:(rbk + 1) * C], pT_f[0:128, 0:C]))
                        W(te, s_v)
                    if DEBUG and j == 0:
                        W(sy, s_v)
                        I(sy, lambda e: e.dma_start(out=dbg["kn0"], in_=kn_tm[:].bitcast(F32)), (s_dbg, 16))
                    W(sy, s_dbg)
                    W(ve, s_dbg)
                    W(sc, s_dbg)
                    W(te, s_dbg)
                    W(gp, s_dbg)

                    for hh in range(2):
                        for sub in range(2):
                            I(te, lambda e, hh=hh, sub=sub: e.matmul(
                                pC_[0:C, 0:C],
                                kn_fm[:, (hh * 2 + sub) * C:(hh * 2 + sub + 1) * C],
                                kn_fm[:, (hh * 2 + sub) * C:(hh * 2 + sub + 1) * C],
                                start=(sub == 0), stop=(sub == 1)), (s_t, 1))
                        W(ve, s_t)
                        vop_(lambda e, hh=hh: e.tensor_scalar(rs_a[0:C, 8 + hh:9 + hh],
                                                              beta_tm[:, hh:hh + 1], -1.0,
                                                              None, ALU.mult))
                        DR(ve)
                        vop_(lambda e, hh=hh: e.scalar_tensor_tensor(
                            Lt[:, hh * C:(hh + 1) * C], pC_[0:C, 0:C],
                            rs_a[0:C, 8 + hh:9 + hh], mask_low[:], ALU.mult, ALU.mult))
                        W(te, s_v)
                        I(te, lambda e, hh=hh: e.transpose(
                            pT_b[0:C, 0:C], Lt[:, hh * C:(hh + 1) * C], ident_b[0:C, 0:C]), (s_t, 1))
                        W(ve, s_t)
                        vop_(lambda e, hh=hh: e.tensor_copy(LtT[:, hh * C:(hh + 1) * C], pT_b[0:C, 0:C]))
                        DR(ve)
                        vop_(lambda e, hh=hh: e.tensor_tensor(
                            PT[:, hh * C:(hh + 1) * C], ident_b[0:C, 0:C],
                            LtT[:, hh * C:(hh + 1) * C], ALU.add))
                        vop_(lambda e, hh=hh: e.tensor_copy(Xp[:, hh * C:(hh + 1) * C],
                                                            Lt[:, hh * C:(hh + 1) * C]))
                        vop_(lambda e, hh=hh: e.tensor_copy(XpT[:, hh * C:(hh + 1) * C],
                                                            LtT[:, hh * C:(hh + 1) * C]))
                        W(te, s_v)
                        for lev in range(5):
                            I(te, lambda e, hh=hh: e.matmul(
                                pC_[0:C, 0:C], XpT[:, hh * C:(hh + 1) * C],
                                Xp[:, hh * C:(hh + 1) * C], start=True, stop=True), (s_t, 1))
                            W(ve, s_t)
                            vop_(lambda e, hh=hh: e.tensor_copy(Xp[:, hh * C:(hh + 1) * C],
                                                                pC_[0:C, 0:C]))
                            W(te, s_v)
                            I(te, lambda e, hh=hh: e.transpose(
                                pT_b[0:C, 0:C], Xp[:, hh * C:(hh + 1) * C], ident_b[0:C, 0:C]), (s_t, 1))
                            I(te, lambda e, hh=hh: e.matmul(
                                pD_[0:C, 0:C], Xp[:, hh * C:(hh + 1) * C],
                                PT[:, hh * C:(hh + 1) * C], start=True, stop=True), (s_t, 1))
                            W(ve, s_t)
                            vop_(lambda e, hh=hh: e.tensor_copy(XpT[:, hh * C:(hh + 1) * C],
                                                                pT_b[0:C, 0:C]))
                            vop_(lambda e, hh=hh: e.tensor_tensor(
                                PT2[:, hh * C:(hh + 1) * C], pD_[0:C, 0:C],
                                PT[:, hh * C:(hh + 1) * C], ALU.add))
                            DR(ve)
                            vop_(lambda e, hh=hh: e.tensor_copy(PT[:, hh * C:(hh + 1) * C],
                                                                PT2[:, hh * C:(hh + 1) * C]))
                            W(te, s_v)
                        vop_(lambda e, hh=hh: e.tensor_copy(PTr[:, hh * C:(hh + 1) * C],
                                                            PT[:, hh * C:(hh + 1) * C]))
                        W(te, s_v)
                        I(te, lambda e, hh=hh: e.matmul(
                            pC_[0:C, 0:DV], PTr[:, hh * C:(hh + 1) * C],
                            bvbk[:, hh * 512:hh * 512 + DV], start=True, stop=True), (s_t, 1))
                        for sub in range(2):
                            I(te, lambda e, hh=hh, sub=sub: e.matmul(
                                pD_[0:128, sub * C:(sub + 1) * C],
                                bvbk[:, hh * 512 + DV + sub * 128: hh * 512 + DV + (sub + 1) * 128],
                                PTr[:, hh * C:(hh + 1) * C], start=True, stop=True), (s_t, 1))
                        W(ve, s_t)
                        vop_(lambda e, hh=hh: e.tensor_copy(u_tm[:, hh * DV:(hh + 1) * DV],
                                                            pC_[0:C, 0:DV]))
                        vop_(lambda e, hh=hh: e.tensor_copy(w_fm[:, (hh * 2) * C:(hh * 2 + 1) * C],
                                                            pD_[0:128, 0:C]))
                        vop_(lambda e, hh=hh: e.tensor_copy(w_fm[:, (hh * 2 + 1) * C:(hh * 2 + 2) * C],
                                                            pD_[0:128, C:2 * C]))
                        W(te, s_v)
                        for sub in range(2):
                            I(te, lambda e, hh=hh, sub=sub, cc=cc: e.matmul(
                                pC_[0:C, 0:C],
                                kn_fm[:, (hh * 2 + sub) * C:(hh * 2 + sub + 1) * C],
                                qfm[:, (hh * 2 + sub) * GRP + cc:(hh * 2 + sub) * GRP + cc + C],
                                start=(sub == 0), stop=(sub == 1)), (s_t, 1))
                        W(ve, s_t)
                        vop_(lambda e, hh=hh: e.tensor_tensor(
                            attnT[:, hh * C:(hh + 1) * C], pC_[0:C, 0:C], mask_upc[:], ALU.mult))
                        W(te, s_v)
                        for sub in range(2):
                            I(te, lambda e, hh=hh, sub=sub: e.matmul(
                                pC_[0:C, 0:DV],
                                w_fm[:, (hh * 2 + sub) * C:(hh * 2 + sub + 1) * C],
                                S_sb[:, hh * 512 + sub * DV: hh * 512 + (sub + 1) * DV],
                                start=(sub == 0), stop=(sub == 1)), (s_t, 1))
                        W(ve, s_t)
                        if j == 0:
                            vop_(lambda e, hh=hh: e.tensor_copy(
                                ui_tm[:, hh * DV:(hh + 1) * DV], u_tm[:, hh * DV:(hh + 1) * DV]))
                        else:
                            vop_(lambda e, hh=hh: e.tensor_tensor(
                                ui_tm[:, hh * DV:(hh + 1) * DV], u_tm[:, hh * DV:(hh + 1) * DV],
                                pC_[0:C, 0:DV], ALU.subtract))
                        W(te, s_v)
                        if j == 0:
                            I(te, lambda e, hh=hh: e.matmul(
                                pD_[0:C, 0:DV], attnT[:, hh * C:(hh + 1) * C],
                                ui_tm[:, hh * DV:(hh + 1) * DV], start=True, stop=True), (s_t, 1))
                        else:
                            for sub in range(2):
                                I(te, lambda e, hh=hh, sub=sub: e.matmul(
                                    pD_[0:C, 0:DV],
                                    q_r[:, (hh * 2 + sub) * C:(hh * 2 + sub + 1) * C],
                                    S_sb[:, hh * 512 + sub * DV: hh * 512 + (sub + 1) * DV],
                                    start=(sub == 0), stop=False), (s_t, 1))
                            I(te, lambda e, hh=hh: e.matmul(
                                pD_[0:C, 0:DV], attnT[:, hh * C:(hh + 1) * C],
                                ui_tm[:, hh * DV:(hh + 1) * DV], start=False, stop=True), (s_t, 1))
                        pS = pS0 if hh == 0 else pS1
                        for sub in range(2):
                            I(te, lambda e, pS=pS, hh=hh, sub=sub, j=j: e.matmul(
                                pS[:, sub * DV:(sub + 1) * DV],
                                kn_tm[:, hh * DV + sub * 128: hh * DV + (sub + 1) * 128],
                                ui_tm[:, hh * DV:(hh + 1) * DV],
                                start=(j == 0), stop=False), (s_t, 1))
                        W(ve, s_t)
                        W(ve, s_ld)
                        vop_(lambda e, hh=hh: e.tensor_scalar(
                            o_tile[:, hh * DV:(hh + 1) * DV], pD_[0:C, 0:DV],
                            rs_b[0:C, 4 + hh:5 + hh], None, ALU.mult))
                        vop_(lambda e, pS=pS, hh=hh: e.tensor_copy(
                            S_sb[:, hh * 512:hh * 512 + 512], pS[:, 0:512]))
                        W(te, s_v)

                    if DEBUG and j == 0:
                        W(sy, s_v)
                        W(gp, s_v)
                        I(sy, lambda e: e.dma_start(
                            out=dbg["uw0"][:, 0:512], in_=u_tm[:].bitcast(F32)), (s_dbg, 16))
                        I(sy, lambda e: e.dma_start(
                            out=dbg["uw0"][:, 512:1024], in_=ui_tm[:].bitcast(F32)), (s_dbg, 16))
                        I(gp, lambda e: e.dma_start(out=dbg["o0"][:],
                                                    in_=o_tile[:]), (s_dbg, 16))
                    W(sy, s_dbg)
                    W(ve, s_dbg)
                    W(sc, s_dbg)
                    W(te, s_dbg)
                    W(gp, s_dbg)
                    W(sy, s_v)
                    ld(o_loc[j * C:(j + 1) * C, :], o_tile[:])


            W(sy, s_ld)
            for piece in range(4):
                if state["cur"] == sy:
                    e = state["eng"]
                    ins = e.dma_start(out=state["rv"]["o_wr"][piece * 2048:(piece + 1) * 2048, :],
                                      in_=o_loc[piece * 2048:(piece + 1) * 2048, :])
                state["cnt"][s_ld.name] = state["cnt"].get(s_ld.name, 0) + 16
                if state["cur"] == sy:
                    ins.then_inc(s_ld, 16)
            W(sy, s_ld)
            if state["cur"] == sy:
                e = state["eng"]
                ins = e.dma_start(out=state["rv"]["fl_wr"][:, 0:1], in_=nonce_sb[0:1, 0:1])
            state["cnt"][s_ld.name] = state["cnt"].get(s_ld.name, 0) + 16
            if state["cur"] == sy:
                ins.then_inc(s_ld, 16)
                pollr, noncer = state["rv"]["pollr"], state["rv"]["noncer"]
                e.reg_mov(pollr, 1)
                wh = e.While(pollr)
                wh.__enter__()
                e.sem_clear(s_p)
                e.dma_start(out=flag_sb[0:1, 0:1],
                            in_=state["rv"]["fl_rd_all"][0:1, 0:1]).then_inc(s_p, 16)
                e.wait_ge(s_p, 16)
                e.reg_load(pollr, flag_sb[0:1, 0:1])
                e.reg_alu(pollr, pollr, noncer, ALU.not_equal)
                wh.__exit__(None, None, None)
            for r in range(2):
                for piece in range(2):
                    if state["cur"] == sy:
                        e = state["eng"]
                        reg = state["rv"]["o_rd0"] if r == 0 else state["rv"]["o_rd1"]
                        ins = e.dma_start(out=o_stage[r * OWN + piece * 2048: r * OWN + (piece + 1) * 2048, :],
                                          in_=reg[piece * 2048:(piece + 1) * 2048, :])
                    state["cnt"][s_ld.name] = state["cnt"].get(s_ld.name, 0) + 16
                    if state["cur"] == sy:
                        ins.then_inc(s_ld, 16)
            W(sy, s_ld)

            load_w(w1_sb, w1T, 8, 2048)
            ld(wstage[0:32, 0:2048], w1T[1024:1056, :])
            W(ve, s_ld)
            vop_(lambda e: e.tensor_copy(w1_sb[0:32, 8 * 2048:9 * 2048], wstage[0:32, 0:2048]))
            W(sy, s_v)
            build_diags(vdiag, cw_sb, 32, 8, 4)
            build_diags(sdiag, firs_sb, 0, 8, 3)
            W(te, s_g)
            W(te, s_v)

            def xfm_own(r0, n_ti):
                for ti_ in range(n_ti):
                    W(sy, s_v)
                    ld(xt_f[:], x_own[r0 + ti_ * 128: r0 + (ti_ + 1) * 128, :])
                    W(ve, s_ld)
                    vop_(lambda e: e.tensor_copy(xt_b[:, 0:D], xt_f[:]))
                    W(sc, s_v)
                    W(sc, s_t)
                    for kk in range(8):
                        dmat(xfm[:, kk * GRP + ti_ * 128: kk * GRP + (ti_ + 1) * 128],
                             xt_b[:, kk * 128:(kk + 1) * 128])
                    W(ve, s_ld2)
                W(te, s_ld2)

            def v_own(n_tok, halo_mode, first=False):
                for rb in range(8):
                    pp = pA if (rb % 2 == 0) else pB
                    for kk in range(8):
                        I(te, lambda e, pp=pp, rb=rb, kk=kk, n_tok=n_tok: e.matmul(
                            pp[:, 0:n_tok],
                            wv_sb[:, kk * 1024 + rb * 128: kk * 1024 + rb * 128 + 128],
                            xfm[:, kk * GRP: kk * GRP + n_tok],
                            start=(kk == 0), stop=(kk == 7)), (s_t, 1))
                    W(ve, s_t)
                    base = rb * (GRP + NH)
                    if halo_mode:
                        vop_(lambda e, base=base: e.memset(vop[:, base:base + NH], 0.0))
                    elif not first:
                        vop_(lambda e, base=base: e.tensor_copy(
                            vop[:, base:base + NH],
                            vop[:, base + GRP: base + GRP + NH]))
                    DR(ve)
                    vop_(lambda e, base=base, pp=pp, n_tok=n_tok: e.tensor_copy(
                        vop[:, base + NH: base + NH + n_tok], pp[:, 0:n_tok]))
                    W(te, s_v)
                    ppc = pC_ if (rb % 2 == 0) else pD_
                    for t in range(4):
                        I(te, lambda e, ppc=ppc, rb=rb, t=t, base=base, n_tok=n_tok: e.matmul(
                            ppc[:, 0:n_tok],
                            vdiag[:, (rb * 4 + t) * 128:(rb * 4 + t) * 128 + 128],
                            vop[:, base + 1 + t: base + 1 + t + n_tok],
                            start=(t == 0), stop=(t == 3)), (s_t, 1))
                    W(sc, s_t)
                    I(sc, lambda e, ppc=ppc, n_tok=n_tok: e.activation(
                        tstage[:, 0:n_tok], ppc[:, 0:n_tok], AF.Tanh, scale=0.5), (s_a, 1))
                    W(ve, s_a)
                    vop_(lambda e, n_tok=n_tok: e.tensor_scalar(
                        tstage[:, GRP:GRP + n_tok], tstage[:, 0:n_tok], 0.5, 0.5, ALU.mult, ALU.add))
                    DR(ve)
                    W(ve, s_t)
                    vbase = rb * (GRP + 64)
                    if halo_mode:
                        vop_(lambda e, n_tok=n_tok, ppc=ppc: e.tensor_tensor(
                            tstage[:, 2 * GRP - 128:2 * GRP], tstage[:, GRP:GRP + n_tok],
                            ppc[:, 0:n_tok], ALU.mult))
                        DR(ve)
                        vop_(lambda e, vbase=vbase: e.tensor_copy(
                            vow[:, vbase:vbase + 64], tstage[:, 2 * GRP - 64:2 * GRP]))
                        vop_(lambda e, base=base: e.tensor_copy(
                            vop[:, base:base + NH],
                            vop[:, base + NH + 128 - NH: base + NH + 128]))
                    else:
                        vop_(lambda e, vbase=vbase, n_tok=n_tok, ppc=ppc: e.tensor_tensor(
                            vow[:, vbase + 64: vbase + 64 + n_tok], tstage[:, GRP:GRP + n_tok],
                            ppc[:, 0:n_tok], ALU.mult))
                    W(te, s_v)

            xfm_own(0, 1)
            v_own(128, True)

            for og in range(NOG_LIM):
                ot0 = og * GRP
                xfm_own(XHALO + ot0, 4)
                v_own(GRP, False, first=(og == 0))

                W(ve, s_t)
                for rb in range(8):
                    vbase = rb * (GRP + 64)
                    vop_(lambda e, vbase=vbase, rb=rb: e.tensor_scalar(
                        fir_a[:], vow[:, vbase + 2: vbase + 2 + GRP],
                        firl_sb[:, rb * 63: rb * 63 + 1], None, ALU.mult))
                    cur, oth = fir_b, fir_a
                    for t in range(1, 63):
                        vop_(lambda e, cur=cur, oth=oth, vbase=vbase, rb=rb, t=t:
                             e.scalar_tensor_tensor(cur[:], vow[:, vbase + 2 + t: vbase + 2 + t + GRP],
                                                    firl_sb[:, rb * 63 + t: rb * 63 + t + 1],
                                                    oth[:], ALU.mult, ALU.add))
                        cur, oth = oth, cur
                    W(ve, s_ld2)
                    vop_(lambda e, oth=oth: e.tensor_copy(fir_o[:], oth[:]))
                    W(sc, s_v)
                    for ti in range(4):
                        dmat(pth_l[:, ti * D + rb * 128: ti * D + (rb + 1) * 128],
                             fir_o[:, ti * 128:(ti + 1) * 128])
                    vop_(lambda e, vbase=vbase, rb=rb: e.tensor_scalar(
                        fir_a[:], vow[:, vbase + 62: vbase + 62 + GRP],
                        firs_sb[:, rb * 3: rb * 3 + 1], None, ALU.mult))
                    DR(ve)
                    vop_(lambda e, vbase=vbase, rb=rb: e.scalar_tensor_tensor(
                        fir_b[:], vow[:, vbase + 63: vbase + 63 + GRP],
                        firs_sb[:, rb * 3 + 1: rb * 3 + 2], fir_a[:], ALU.mult, ALU.add))
                    DR(ve)
                    vop_(lambda e, vbase=vbase, rb=rb: e.scalar_tensor_tensor(
                        fir_a[:], vow[:, vbase + 64: vbase + 64 + GRP],
                        firs_sb[:, rb * 3 + 2: rb * 3 + 3], fir_b[:], ALU.mult, ALU.add))
                    DR(ve)
                    W(ve, s_ld2)
                    vop_(lambda e: e.tensor_copy(fir_o[:], fir_a[:]))
                    W(sc, s_v)
                    for ti in range(4):
                        dmat(pth_s[:, ti * D + rb * 128: ti * D + (rb + 1) * 128],
                             fir_o[:, ti * 128:(ti + 1) * 128])
                        dmat(pth_v[:, ti * D + rb * 128: ti * D + (rb + 1) * 128],
                             vow[:, vbase + 64 + ti * 128: vbase + 64 + (ti + 1) * 128])
                    vop_(lambda e, vbase=vbase: e.tensor_copy(
                        vow[:, vbase:vbase + 64], vow[:, vbase + GRP: vbase + GRP + 64]))

                W(sy, s_v)
                for ti in range(4):
                    for r in range(2):
                        row = r * OWN + og * GRP + ti * 128
                        ld(pth_d[:, ti * D + r * 512: ti * D + r * 512 + 512],
                           o_stage[row:row + 128, :])
                W(ve, s_ld)
                W(ve, s_ld2)
                W(te, s_ld)
                W(te, s_ld2)

                for ti in range(4):
                    tcol = ti * D
                    for pi, pth in enumerate([pth_s, pth_l, pth_d, pth_v]):
                        vop_(lambda e, pth=pth, tcol=tcol, ti=ti, pi=pi: e.tensor_reduce(
                            stat_tm[:, ti * 32 + 8 * pi: ti * 32 + 8 * pi + 8:2],
                            pth[:, tcol:tcol + D].rearrange("p (h dv) -> p h dv", h=4),
                            AX.X, ALU.add))
                        W(sc, s_v)
                        I(sc, lambda e, pth=pth, tcol=tcol: e.activation(
                            sq_t[:, 0:D], pth[:, tcol:tcol + D], AF.Square), (s_a, 1))
                        W(ve, s_a)
                        vop_(lambda e, ti=ti, pi=pi: e.tensor_reduce(
                            stat_tm[:, ti * 32 + 8 * pi + 1: ti * 32 + 8 * pi + 8:2],
                            sq_t[:, 0:D].rearrange("p (h dv) -> p h dv", h=4),
                            AX.X, ALU.add))
                        DR(ve)
                    vop_(lambda e, ti=ti: e.tensor_scalar(sm_a[:, 0:32], stat_tm[:, ti * 32: ti * 32 + 32],
                                                          1.0 / DV, None, ALU.mult))
                    DR(ve)
                    vop_(lambda e: e.tensor_tensor(sm_b[:, 0:32], sm_a[:, 0:32], sm_a[:, 0:32], ALU.mult))
                    DR(ve)
                    vop_(lambda e, ti=ti: e.tensor_tensor(
                        stat_tm[:, ti * 32 + 1: ti * 32 + 32: 2],
                        sm_a[:, 1:32:2], sm_b[:, 0:32:2], ALU.subtract))
                    vop_(lambda e, ti=ti: e.tensor_copy(
                        stat_tm[:, ti * 32: ti * 32 + 32: 2], sm_a[:, 0:32:2]))
                    DR(ve)
                if DEBUG and og == 0:
                    W(sy, s_v)
                    W(gp, s_v)
                    W(gp, s_ld2)
                    I(sy, lambda e: e.dma_start(out=dbg["stat"], in_=stat_tm[:]), (s_dbg, 16))
                    I(gp, lambda e: e.dma_start(out=dbg["pths"][:, 0:D], in_=pth_s[:, 0:D]), (s_dbg, 16))
                    I(gp, lambda e: e.dma_start(out=dbg["pths"][:, D:2 * D], in_=pth_l[:, 0:D]), (s_dbg, 16))
                    I(gp, lambda e: e.dma_start(out=dbg["pths"][:, 2 * D:3 * D], in_=pth_d[:, 0:D]), (s_dbg, 16))
                    I(gp, lambda e: e.dma_start(out=dbg["pths"][:, 3 * D:4 * D], in_=pth_v[:, 0:D]), (s_dbg, 16))
                    W(sy, s_dbg)
                    W(ve, s_dbg)
                    W(sc, s_dbg)
                    W(te, s_dbg)
                    W(gp, s_dbg)

                W(te, s_v)
                for ti in range(4):
                    I(te, lambda e, ti=ti: e.transpose(
                        pT_f[0:32, 0:128], stat_tm[:, ti * 32:(ti + 1) * 32], ident_f[:]), (s_t, 1))
                    W(ve, s_t)
                    vop_(lambda e, ti=ti: e.tensor_copy(
                        stat_fm[0:32, ti * 128:(ti + 1) * 128], pT_f[0:32, 0:128]))
                    W(te, s_v)

                for m in range(16):
                    pp = pA if (m % 2 == 0) else pB
                    for kk in range(8):
                        I(te, lambda e, pp=pp, m=m, kk=kk: e.matmul(
                            pp[:, 0:GRP], w1_sb[:, kk * 2048 + m * 128: kk * 2048 + m * 128 + 128],
                            xfm[:, kk * GRP:(kk + 1) * GRP],
                            start=(kk == 0), stop=False), (s_t, 1))
                    I(te, lambda e, pp=pp, m=m: e.matmul(
                        pp[:, 0:GRP], w1_sb[0:32, 8 * 2048 + m * 128: 8 * 2048 + m * 128 + 128],
                        stat_fm[0:32, 0:GRP], start=False, stop=True), (s_t, 1))
                    W(sc, s_t)
                    I(sc, lambda e, pp=pp, m=m: e.activation(
                        hmid[:, (m % 2) * GRP:(m % 2) * GRP + GRP], pp[:, 0:GRP],
                        AF.Gelu, bias=gb1_sb[:, m:m + 1]), (s_a, 1))
                    W(te, s_a)
                    I(te, lambda e, m=m: e.matmul(
                        pC_[0:16, 0:GRP], w2_sb[:, m * 16:(m + 1) * 16],
                        hmid[:, (m % 2) * GRP:(m % 2) * GRP + GRP],
                        start=(m == 0), stop=(m == 15)), (s_t, 1))
                W(ve, s_t)
                vop_(lambda e: e.tensor_scalar(logit_f[:, 0:GRP], pC_[0:16, 0:GRP],
                                               bb_sb[0:16, 0:1], None, ALU.add))
                W(te, s_v)
                for ti in range(4):
                    I(te, lambda e, ti=ti: e.transpose(
                        pT_f[0:128, 0:16], logit_f[:, ti * 128:(ti + 1) * 128],
                        ident_f[0:16, 0:16]), (s_t, 1))
                    W(ve, s_t)
                    vop_(lambda e, ti=ti: e.tensor_copy(logit_t[:, ti * 16:(ti + 1) * 16],
                                                        pT_f[0:128, 0:16]))
                    W(te, s_v)
                DR(ve)
                vop_(lambda e: e.tensor_tensor(sm_a[:, 0:64], logit_t[:, 0:64],
                                               itemp_sb[:, 0:64], ALU.mult))
                DR(ve)
                vop_(lambda e: e.tensor_reduce(sm_b[:, 0:16],
                                               sm_a[:, 0:64].rearrange("p (t q) -> p t q", q=4),
                                               AX.X, ALU.max))
                DR(ve)
                for th in range(16):
                    vop_(lambda e, th=th: e.tensor_scalar(
                        prob_t[:, th * 4:(th + 1) * 4], sm_a[:, th * 4:(th + 1) * 4],
                        sm_b[:, th:th + 1], None, ALU.subtract))
                W(sc, s_v)
                I(sc, lambda e: e.activation(sm_a[:, 0:64], prob_t[:, 0:64], AF.Tanh, scale=0.5), (s_a, 1))
                W(ve, s_a)
                vop_(lambda e: e.tensor_scalar(sm_b[:, 16:80], sm_a[:, 0:64], -1.0, 1.0,
                                               ALU.mult, ALU.add))
                DR(ve)
                vop_(lambda e: e.reciprocal(prob_t[:, 0:64], sm_b[:, 16:80]))
                DR(ve)
                vop_(lambda e: e.tensor_scalar(sm_b[:, 16:80], sm_a[:, 0:64], 1.0, None, ALU.add))
                DR(ve)
                vop_(lambda e: e.tensor_tensor(sm_a[:, 0:64], sm_b[:, 16:80], prob_t[:, 0:64], ALU.mult))
                DR(ve)
                vop_(lambda e: e.tensor_reduce(sm_b[:, 0:16],
                                               sm_a[:, 0:64].rearrange("p (t q) -> p t q", q=4),
                                               AX.X, ALU.add))
                DR(ve)
                vop_(lambda e: e.reciprocal(sm_b[:, 16:32], sm_b[:, 0:16]))
                DR(ve)
                for th in range(16):
                    vop_(lambda e, th=th: e.tensor_scalar(
                        prob_t[:, th * 4:(th + 1) * 4], sm_a[:, th * 4:(th + 1) * 4],
                        sm_b[:, 16 + th:17 + th], None, ALU.mult))
                DR(ve)
                vop_(lambda e: e.tensor_tensor(sm_a[:, 0:64], prob_t[:, 0:64],
                                               floor_sb[:, 0:64], ALU.max))
                DR(ve)
                vop_(lambda e: e.tensor_reduce(sm_b[:, 0:16],
                                               sm_a[:, 0:64].rearrange("p (t q) -> p t q", q=4),
                                               AX.X, ALU.add))
                DR(ve)
                vop_(lambda e: e.reciprocal(sm_b[:, 16:32], sm_b[:, 0:16]))
                DR(ve)
                for th in range(16):
                    vop_(lambda e, th=th: e.tensor_scalar(
                        prob_t[:, th * 4:(th + 1) * 4], sm_a[:, th * 4:(th + 1) * 4],
                        sm_b[:, 16 + th:17 + th], None, ALU.mult))
                DR(ve)
                if DEBUG and og == 0:
                    W(sy, s_v)
                    I(sy, lambda e: e.dma_start(out=dbg["probs"], in_=prob_t[:]), (s_dbg, 16))
                    W(sy, s_dbg)
                    W(ve, s_dbg)
                    W(sc, s_dbg)
                    W(te, s_dbg)
                    W(gp, s_dbg)

                for ti in range(4):
                    tcol = ti * D
                    for hh in range(4):
                        vop_(lambda e, hh=hh, tcol=tcol, ti=ti: e.tensor_scalar(
                            comb_a[:, hh * DV:(hh + 1) * DV],
                            pth_s[:, tcol + hh * DV: tcol + (hh + 1) * DV],
                            prob_t[:, ti * 16 + hh * 4: ti * 16 + hh * 4 + 1], None, ALU.mult))
                    DR(ve)
                    for hh in range(4):
                        vop_(lambda e, hh=hh, tcol=tcol, ti=ti: e.scalar_tensor_tensor(
                            comb_b[:, hh * DV:(hh + 1) * DV],
                            pth_l[:, tcol + hh * DV: tcol + (hh + 1) * DV],
                            prob_t[:, ti * 16 + hh * 4 + 1: ti * 16 + hh * 4 + 2],
                            comb_a[:, hh * DV:(hh + 1) * DV], ALU.mult, ALU.add))
                    DR(ve)
                    for hh in range(4):
                        vop_(lambda e, hh=hh, tcol=tcol, ti=ti: e.scalar_tensor_tensor(
                            comb_a[:, hh * DV:(hh + 1) * DV],
                            pth_d[:, tcol + hh * DV: tcol + (hh + 1) * DV],
                            prob_t[:, ti * 16 + hh * 4 + 2: ti * 16 + hh * 4 + 3],
                            comb_b[:, hh * DV:(hh + 1) * DV], ALU.mult, ALU.add))
                    DR(ve)
                    for hh in range(4):
                        vop_(lambda e, hh=hh, tcol=tcol, ti=ti: e.scalar_tensor_tensor(
                            comb_b[:, hh * DV:(hh + 1) * DV],
                            pth_v[:, tcol + hh * DV: tcol + (hh + 1) * DV],
                            prob_t[:, ti * 16 + hh * 4 + 3: ti * 16 + hh * 4 + 4],
                            comb_a[:, hh * DV:(hh + 1) * DV], ALU.mult, ALU.add))
                    DR(ve)
                    vop_(lambda e: e.tensor_tensor(sq_t[:, 0:D], comb_b[:, 0:D], comb_b[:, 0:D], ALU.mult))
                    DR(ve)
                    vop_(lambda e: e.tensor_reduce(ln_sc[:, 0:4],
                                                   sq_t[:, 0:D].rearrange("p (h dv) -> p h dv", h=4),
                                                   AX.X, ALU.add))
                    DR(ve)
                    vop_(lambda e: e.tensor_scalar(rs_a[:, 0:4], ln_sc[:, 0:4], 1.0 / DV, 1e-5,
                                                   ALU.mult, ALU.add))
                    DR(ve)

                    def newton4(xcols, ycols, n=4):
                        x0, x1_ = xcols
                        y0, y1_ = ycols
                        vop_(lambda e: e.tensor_scalar(rs_i[:, 0:n], rs_a[:, x0:x1_].bitcast(I32),
                                                       10, None, ALU.logical_shift_right))
                        DR(ve)
                        vop_(lambda e: e.tensor_copy(rs_sc[:, 0:n], rs_i[:, 0:n]))
                        DR(ve)
                        vop_(lambda e: e.tensor_scalar(rs_sc[:, 4:4 + n], rs_sc[:, 0:n], MAGIC_HI,
                                                       -1.0, ALU.subtract, ALU.mult))
                        DR(ve)
                        vop_(lambda e: e.tensor_copy(rs_i[:, 4:4 + n], rs_sc[:, 4:4 + n]))
                        DR(ve)
                        vop_(lambda e: e.tensor_scalar(rs_i[:, 8:8 + n], rs_i[:, 4:4 + n], 9, None,
                                                       ALU.logical_shift_left))
                        DR(ve)
                        vop_(lambda e: e.tensor_copy(rs_b[:, y0:y1_], rs_i[:, 8:8 + n].bitcast(F32)))
                        DR(ve)
                        for it in range(3):
                            vop_(lambda e: e.tensor_tensor(rs_sc[:, 0:n], rs_b[:, y0:y1_],
                                                           rs_b[:, y0:y1_], ALU.mult))
                            DR(ve)
                            vop_(lambda e: e.tensor_tensor(rs_sc[:, 4:4 + n], rs_sc[:, 0:n],
                                                           rs_a[:, x0:x1_], ALU.mult))
                            DR(ve)
                            vop_(lambda e: e.tensor_scalar(rs_sc[:, 8:8 + n], rs_sc[:, 4:4 + n],
                                                           -0.5, 1.5, ALU.mult, ALU.add))
                            DR(ve)
                            vop_(lambda e: e.tensor_tensor(rs_sc[:, 12:12 + n], rs_b[:, y0:y1_],
                                                           rs_sc[:, 8:8 + n], ALU.mult))
                            DR(ve)
                            vop_(lambda e: e.tensor_copy(rs_b[:, y0:y1_], rs_sc[:, 12:12 + n]))
                            DR(ve)

                    newton4((0, 4), (0, 4))
                    W(ve, s_ld2)
                    for hh in range(4):
                        vop_(lambda e, hh=hh: e.tensor_scalar(
                            orms[:, hh * DV:(hh + 1) * DV], comb_b[:, hh * DV:(hh + 1) * DV],
                            rs_b[:, hh:hh + 1], None, ALU.mult))
                    W(sc, s_v)
                    W(sc, s_t)
                    for kk in range(8):
                        dmat(ofm[:, kk * 128:(kk + 1) * 128], orms[:, kk * 128:(kk + 1) * 128])
                    W(te, s_ld2)
                    for half in range(2):
                        pp = pA if half == 0 else pB
                        for kk in range(8):
                            I(te, lambda e, pp=pp, half=half, kk=kk: e.matmul(
                                pp[:, 0:512], ofm[:, kk * 128:(kk + 1) * 128],
                                wo_sb[:, kk * 1024 + half * 512: kk * 1024 + half * 512 + 512],
                                start=(kk == 0), stop=(kk == 7)), (s_t, 1))
                    W(sy, s_v)
                    ld(xt_f[:], x_own[XHALO + ot0 + ti * 128: XHALO + ot0 + (ti + 1) * 128, :])
                    W(ve, s_t)
                    W(ve, s_ld)
                    vop_(lambda e: e.tensor_tensor(y_t[:, 0:512], pA[:, 0:512], xt_f[:, 0:512], ALU.add))
                    vop_(lambda e: e.tensor_tensor(y_t[:, 512:1024], pB[:, 0:512],
                                                   xt_f[:, 512:1024], ALU.add))
                    W(te, s_v)
                    W(ve, s_a)
                    DR(ve)
                    vop_(lambda e: e.tensor_reduce(ln_sc[:, 0:1], y_t[:, 0:D], AX.X, ALU.add))
                    vop_(lambda e: e.tensor_tensor(tstage[:, 0:GRP], y_t[:, 0:GRP], y_t[:, 0:GRP], ALU.mult))
                    vop_(lambda e: e.tensor_tensor(tstage[:, GRP:2 * GRP], y_t[:, GRP:D],
                                                   y_t[:, GRP:D], ALU.mult))
                    DR(ve)
                    vop_(lambda e: e.tensor_reduce(ln_sc[:, 1:2], tstage[:, 0:2 * GRP], AX.X, ALU.add))
                    DR(ve)
                    vop_(lambda e: e.tensor_scalar(ln_sc[:, 2:3], ln_sc[:, 0:1], 1.0 / D, None, ALU.mult))
                    DR(ve)
                    vop_(lambda e: e.tensor_tensor(ln_sc[:, 3:4], ln_sc[:, 2:3], ln_sc[:, 2:3], ALU.mult))
                    vop_(lambda e: e.tensor_scalar(ln_sc[:, 4:5], ln_sc[:, 1:2], 1.0 / D, None, ALU.mult))
                    DR(ve)
                    vop_(lambda e: e.tensor_tensor(ln_sc[:, 5:6], ln_sc[:, 4:5], ln_sc[:, 3:4], ALU.subtract))
                    DR(ve)
                    vop_(lambda e: e.tensor_scalar(rs_a[:, 0:1], ln_sc[:, 5:6], 1.0, 1e-5, ALU.mult, ALU.add))
                    DR(ve)
                    newton4((0, 1), (0, 1), n=1)
                    vop_(lambda e: e.tensor_scalar(ln_sc[:, 10:11], ln_sc[:, 2:3], -1.0, None, ALU.mult))
                    DR(ve)
                    W(ve, s_ld)
                    vop_(lambda e: e.tensor_scalar(x1_t[:, 0:D], y_t[:, 0:D],
                                                   ln_sc[:, 10:11], rs_b[:, 0:1], ALU.add, ALU.mult))
                    DR(ve)
                    vop_(lambda e: e.tensor_reduce(ln_sc[:, 12:13], x1_t[:, 0:D], AX.X, ALU.max,
                                                   apply_absolute_value=True))
                    DR(ve)
                    vop_(lambda e: e.tensor_tensor(ln_sc[:, 13:14], amax_run[:, 0:1],
                                                   ln_sc[:, 12:13], ALU.max))
                    DR(ve)
                    vop_(lambda e: e.tensor_copy(amax_run[:, 0:1], ln_sc[:, 13:14]))
                    W(sy, s_v)
                    ld(x1_spill[ot0 + ti * 128: ot0 + (ti + 1) * 128, :], x1_t[:, 0:D])
                    if DEBUG and og == 0 and ti == 0:
                        I(sy, lambda e: e.dma_start(out=dbg["x1"], in_=x1_t[:]), (s_dbg, 16))
                    W(sy, s_dbg)
                    W(ve, s_dbg)
                    W(sc, s_dbg)
                    W(te, s_dbg)
                    W(gp, s_dbg)

            W(te, s_v)
            I(te, lambda e: e.transpose(pT_f[0:1, 0:128], amax_run[:, 0:1], ident_f[:]), (s_t, 1))
            W(ve, s_t)
            vop_(lambda e: e.tensor_reduce(sA[0:1, 40:41], pT_f[0:1, 0:128], AX.X, ALU.max))
            DR(ve)
            W(sy, s_v)
            if state["cur"] == sy:
                e = state["eng"]
                e.dma_start(out=state["rv"]["sc_wr"][:, 0:1], in_=sA[0:1, 40:41]).then_inc(s_ld, 16)
            state["cnt"][s_ld.name] = state["cnt"].get(s_ld.name, 0) + 16
            W(sy, s_ld)
            if state["cur"] == sy:
                e = state["eng"]
                e.dma_start(out=state["rv"]["flam_wr"][:, 0:1], in_=nonce_sb[0:1, 0:1]).then_inc(s_ld, 16)
                pollr, noncer = state["rv"]["pollr"], state["rv"]["noncer"]
                e.reg_mov(pollr, 1)
                wh = e.While(pollr)
                wh.__enter__()
                e.sem_clear(s_p)
                e.dma_start(out=flag_sb[0:1, 0:1], in_=state["rv"]["flam_rd"][:, 0:1]).then_inc(s_p, 16)
                e.wait_ge(s_p, 16)
                e.reg_load(pollr, flag_sb[0:1, 0:1])
                e.reg_alu(pollr, pollr, noncer, ALU.not_equal)
                wh.__exit__(None, None, None)
                e.dma_start(out=sA[0:1, 41:42], in_=state["rv"]["sc_rd"][:, 0:1]).then_inc(s_ld, 16)
            state["cnt"][s_ld.name] = state["cnt"].get(s_ld.name, 0) + 32
            W(ve, s_ld)
            vop_(lambda e: e.tensor_tensor(sA[0:1, 42:43], sA[0:1, 40:41], sA[0:1, 41:42], ALU.max))
            DR(ve)
            vop_(lambda e: e.tensor_scalar(sA[0:1, 43:44], sA[0:1, 42:43], 1e-12, None, ALU.max))
            DR(ve)
            vop_(lambda e: e.reciprocal(sA[0:1, 44:45], sA[0:1, 43:44]))
            DR(ve)
            vop_(lambda e: e.tensor_scalar(sA[0:1, 45:46], sA[0:1, 44:45], 448.0, 448.0,
                                           ALU.mult, ALU.min))
            DR(ve)
            vop_(lambda e: e.reciprocal(sA[0:1, 46:47], sA[0:1, 45:46]))
            DR(ve)
            vop_(lambda e: e.tensor_scalar(sA[0:1, 47:48], sA[0:1, 45:46], 0.5, None, ALU.mult))
            DR(ve)
            vop_(lambda e: e.tensor_tensor(sA[0:1, 48:49], sA[0:1, 46:47], sA[0:1, 7:8], ALU.mult))
            DR(ve)
            vop_(lambda e: e.tensor_scalar(sA[0:1, 49:50], sA[0:1, 48:49], 4.0, None, ALU.mult))
            DR(ve)
            W(sy, s_v)
            ld(loc_sc[0:1, 4:6], sA[0:1, 47:49][0:1, 0:2] if False else sA[0:1, 47:49])
            W(sy, s_ld)
            ld(loc_sc[0:1, 6:7], sA[0:1, 49:50])
            W(sy, s_ld)
            ld(sc_bc[:, 0:1], loc_sc[0:1, 4:5].broadcast_to((128, 1)))
            ld(sc_bc[:, 1:2], loc_sc[0:1, 6:7].broadcast_to((128, 1)))
            W(ve, s_ld)
            if DEBUG:
                W(sy, s_v)
                I(sy, lambda e: e.dma_start(out=dbg["amax"], in_=amax_run[:]), (s_dbg, 16))

            for kk in range(8):
                ld(wstage[:, 0:1024], wfT[kk * 128:(kk + 1) * 128, :])
                W(ve, s_ld)
                vop_(lambda e, kk=kk: e.tensor_scalar(
                    wf8_sb[:, kk * 1024:(kk + 1) * 1024], wstage[:, 0:1024],
                    ws2_bc[:, 0:1], None, ALU.mult))
                W(sy, s_v)
            for og in range(NOG_LIM):
                for ti in range(4):
                    r0 = og * GRP + ti * 128
                    W(sy, s_v)
                    ld(xt_f[:], x1_spill[r0:r0 + 128, :])
                    W(ve, s_ld)
                    W(ve, s_ld2)
                    vop_(lambda e: e.tensor_scalar(xs_b[:, 0:D], xt_f[:], sc_bc[:, 0:1], None, ALU.mult))
                    W(sc, s_v)
                    W(sc, s_t)
                    for kk in range(8):
                        dmat(xs_fmb[:, kk * 128:(kk + 1) * 128], xs_b[:, kk * 128:(kk + 1) * 128])
                    W(ve, s_ld2)
                    vop_(lambda e: e.tensor_copy(xs_fm8[:], xs_fmb[:]))
                    W(te, s_v)
                    for half in range(2):
                        pp = pA if half == 0 else pB
                        for kk in range(8):
                            I(te, lambda e, pp=pp, half=half, kk=kk: e.matmul(
                                pp[:, 0:512], xs_fm8[:, kk * 128:(kk + 1) * 128],
                                wf8_sb[:, kk * 1024 + half * 512: kk * 1024 + half * 512 + 512],
                                start=(kk == 0), stop=(kk == 7)), (s_t, 1))
                    W(ve, s_t)
                    W(ve, s_gd)
                    vop_(lambda e: e.scalar_tensor_tensor(ffn_t[:, 0:512], pA[:, 0:512],
                                                          sc_bc[:, 1:2], fbias_sb[:, 0:512],
                                                          ALU.mult, ALU.add))
                    vop_(lambda e: e.scalar_tensor_tensor(ffn_t[:, 512:1024], pB[:, 0:512],
                                                          sc_bc[:, 1:2], fbias_sb[:, 512:1024],
                                                          ALU.mult, ALU.add))
                    W(te, s_v)
                    DR(ve)
                    vop_(lambda e: e.tensor_tensor(y_t[:, 0:D], xt_f[:], ffn_t[:, 0:D], ALU.add))
                    DR(ve)
                    vop_(lambda e: e.tensor_reduce(ln_sc[:, 0:1], y_t[:, 0:D], AX.X, ALU.add))
                    vop_(lambda e: e.tensor_tensor(tstage[:, 0:GRP], y_t[:, 0:GRP], y_t[:, 0:GRP], ALU.mult))
                    vop_(lambda e: e.tensor_tensor(tstage[:, GRP:2 * GRP], y_t[:, GRP:D],
                                                   y_t[:, GRP:D], ALU.mult))
                    DR(ve)
                    vop_(lambda e: e.tensor_reduce(ln_sc[:, 1:2], tstage[:, 0:2 * GRP], AX.X, ALU.add))
                    DR(ve)
                    vop_(lambda e: e.tensor_scalar(ln_sc[:, 2:3], ln_sc[:, 0:1], 1.0 / D, None, ALU.mult))
                    DR(ve)
                    vop_(lambda e: e.tensor_tensor(ln_sc[:, 3:4], ln_sc[:, 2:3], ln_sc[:, 2:3], ALU.mult))
                    vop_(lambda e: e.tensor_scalar(ln_sc[:, 4:5], ln_sc[:, 1:2], 1.0 / D, None, ALU.mult))
                    DR(ve)
                    vop_(lambda e: e.tensor_tensor(ln_sc[:, 5:6], ln_sc[:, 4:5], ln_sc[:, 3:4], ALU.subtract))
                    DR(ve)
                    vop_(lambda e: e.tensor_scalar(rs_a[:, 0:1], ln_sc[:, 5:6], 1.0, 1e-5, ALU.mult, ALU.add))
                    DR(ve)
                    newton4((0, 1), (0, 1), n=1)
                    vop_(lambda e: e.tensor_scalar(ln_sc[:, 10:11], ln_sc[:, 2:3], -1.0, None, ALU.mult))
                    DR(ve)
                    W(ve, s_st)
                    vop_(lambda e: e.tensor_scalar(out_t[:, 0:D], y_t[:, 0:D],
                                                   ln_sc[:, 10:11], rs_b[:, 0:1], ALU.add, ALU.mult))
                    W(sy, s_v)
                    I(sy, lambda e, r0=r0: e.dma_start(out=out_d[r0:r0 + 128, :], in_=out_t[:, 0:D]),
                      (s_st, 16))

            W(sy, s_st)
            W(sy, s_ld)
            W(sy, s_ld2)


        @block.sync
        def _(sync):
            emit("sync", sync)

        @block.tensor
        def _(tensor):
            emit("tensor", tensor)

        @block.vector
        def _(vector):
            emit("vector", vector)

        @block.scalar
        def _(scalar):
            emit("scalar", scalar)

        @block.gpsimd
        def _(gpsimd):
            emit("gpsimd", gpsimd)

    return nc, out_d


_cache = {}


def kernel(**inputs):
    x = np.ascontiguousarray(inputs["L0_attn_hidden_states"], dtype=np.float32)
    Wq = np.asarray(inputs["L0_attn_Wq"], np.float32)
    Wk = np.asarray(inputs["L0_attn_Wk"], np.float32)
    Wv = np.asarray(inputs["L0_attn_Wv"], np.float32)
    Wb = np.asarray(inputs["L0_attn_Wb"], np.float32)
    qconv = np.asarray(inputs["L0_attn_qconv_w"], np.float32)
    kconv = np.asarray(inputs["L0_attn_kconv_w"], np.float32)
    vconv = np.asarray(inputs["L0_attn_vconv_w"], np.float32)
    fir_s = np.asarray(inputs["L0_attn_fir_short_w"], np.float32).reshape(D, 3)
    fir_l = np.asarray(inputs["L0_attn_fir_long_w"], np.float32).reshape(D, 63)
    g_w1 = np.asarray(inputs["L0_attn_gate_w1"], np.float32)
    g_b1 = np.asarray(inputs["L0_attn_gate_b1"], np.float32)
    g_w2 = np.asarray(inputs["L0_attn_gate_w2"], np.float32)
    log_temp = np.asarray(inputs["L0_attn_log_temp"], np.float32)
    base_bias = np.asarray(inputs["L0_attn_base_bias"], np.float32)
    floor_raw = np.asarray(inputs["L0_attn_floor_raw"], np.float32)
    onorm_w = np.asarray(inputs["L0_attn_onorm_w"], np.float32)
    Wo = np.asarray(inputs["L0_attn_Wo"], np.float32)
    Wf = np.asarray(inputs["L0_ffn_weight"], np.float32)
    fb = np.asarray(inputs["L0_ffn_bias"], np.float32)

    if "nc" not in _cache:
        _cache["nc"] = build_program()
    nc, out_d = _cache["nc"]

    def rbmajor(a):
        rb = a.shape[0] // 128
        return np.ascontiguousarray(a.reshape(rb, 128, -1).transpose(1, 0, 2).reshape(128, -1))

    nonce = np.frombuffer(os.urandom(8), np.int32).reshape(1, 2).copy()
    nonce[0, 0] |= 1

    in_maps = []
    for c in range(8):
        b, me = c // 2, c % 2
        hp0 = 512 * me
        xb_seq = x[b]
        xo = np.zeros((OWN + XHALO, D), np.float32)
        lo = me * OWN - XHALO
        src0 = max(lo, 0)
        xo[src0 - lo:] = xb_seq[src0: me * OWN + OWN]
        in_maps.append({
            "x_seq": xb_seq,
            "x_own": xo,
            "wqT": np.ascontiguousarray(Wq.T[:, hp0:hp0 + 512]),
            "wkT": np.ascontiguousarray(Wk.T[:, hp0:hp0 + 512]),
            "wbT": np.ascontiguousarray(Wb.T[:, 2 * me:2 * me + 2]),
            "wvT": np.ascontiguousarray(Wv.T),
            "wvsT": np.ascontiguousarray(Wv.T[:, hp0:hp0 + 512]),
            "w1T": np.ascontiguousarray(g_w1.T),
            "w2T": np.ascontiguousarray(g_w2.T),
            "woT": np.ascontiguousarray(Wo.T),
            "wfT": np.ascontiguousarray(Wf.T),
            "qcw": rbmajor(qconv[hp0:hp0 + 512]),
            "kcw": rbmajor(kconv[hp0:hp0 + 512]),
            "vcw": rbmajor(vconv),
            "vscw": rbmajor(vconv[hp0:hp0 + 512]),
            "firs": rbmajor(fir_s),
            "firl": rbmajor(fir_l),
            "gb1": rbmajor(g_b1.reshape(2048, 1)),
            "bb": np.ascontiguousarray(base_bias.reshape(16, 1)),
            "ltemp": log_temp.reshape(1, 4),
            "fraw": floor_raw.reshape(1, 16),
            "onw4": rbmajor(np.tile(onorm_w, H).reshape(D, 1)),
            "fbias": fb,
            "nonce": nonce,
            "meid": np.array([[me]], np.int32),
        })

    res = run_bass_kernel_spmd(nc, in_maps, core_ids=list(range(8)))
    out = np.empty((B, L, D), np.float32)
    for c in range(8):
        b, me = c // 2, c % 2
        out[b, me * OWN:(me + 1) * OWN] = res.results[c]["out"]
    return out


# version 05
# speedup vs baseline: 1.0002x; 1.0002x over previous
import os
import numpy as np
import concourse.bass as bass
import concourse.mybir as mybir
from concourse.bass_utils import run_bass_kernel_spmd

F32 = mybir.dt.float32
F32R = mybir.dt.float32r
BF16 = mybir.dt.bfloat16
FP8 = mybir.dt.float8e4
I32 = mybir.dt.int32
AF = mybir.ActivationFunctionType
ALU = mybir.AluOpType
AX = mybir.AxisListType
ts = bass.ts

B, L, D, H = 4, 8192, 1024, 4
DV = 256
OWN = 4096
C = 64
NCH = L // C
GRP = 512
NG = L // GRP
NOG = OWN // GRP
XHALO = 128
MAGIC_HI = float(0x5F3759DF >> 9)
NH = 4

DEBUG = bool(int(os.environ.get("KBG_DEBUG", "0")))
NG_LIM = int(os.environ.get("KBG_NG", str(NG)))
NOG_LIM = int(os.environ.get("KBG_NOG", str(NOG)))


def build_program():
    nc = bass.Bass("TRN2", target_bir_lowering=False)

    d = {}
    def din(name, shape, dt=F32):
        d[name] = nc.dram_tensor(name, shape, dt, kind="ExternalInput").ap()
        return d[name]

    x_seq = din("x_seq", [L, D])
    x_own = din("x_own", [OWN + XHALO, D])
    wqT = din("wqT", [D, 512]); wkT = din("wkT", [D, 512]); wbT = din("wbT", [D, 2])
    wvT = din("wvT", [D, D]); wvsT = din("wvsT", [D, 512])
    w1T = din("w1T", [1056, 2048]); w2T = din("w2T", [2048, 16])
    woT = din("woT", [D, D]); wfT = din("wfT", [D, D])
    qcw = din("qcw", [128, 16]); kcw = din("kcw", [128, 16])
    vcw = din("vcw", [128, 32]); vscw = din("vscw", [128, 16])
    firs = din("firs", [128, 24]); firl = din("firl", [128, 504])
    gb1 = din("gb1", [128, 16]); bb = din("bb", [16, 1])
    ltemp = din("ltemp", [1, 4]); fraw = din("fraw", [1, 16])
    onw4 = din("onw4", [128, 8]); fbias = din("fbias", [D])
    nonce = din("nonce", [1, 2], I32); meid = din("meid", [1, 1], I32)

    out_d = nc.dram_tensor("out", [OWN, D], F32, kind="ExternalOutput").ap()
    dbg = {}
    if DEBUG:
        for nm, shp in [("q_fm", [512, GRP]), ("o0", [C, 512]), ("x1", [128, D]),
                        ("probs", [128, 64]), ("pths", [128, 4 * D]), ("amax", [128, 4]),
                        ("kn0", [C, 512]), ("uw0", [C, 1024]), ("stat", [128, 128]),
                        ("xfm", [128, 8 * GRP]), ("qpre", [128, GRP + 4])]:
            dbg[nm] = nc.dram_tensor("dbg_" + nm, shp, F32, kind="ExternalOutput").ap()

    sh_o = nc.dram_tensor("sh_o", [2 * L, 512], BF16, kind="Internal", addr_space="Shared").ap()
    sh_flag = nc.dram_tensor("sh_flag", [8, 16], I32, kind="Internal", addr_space="Shared").ap()
    sh_sc = nc.dram_tensor("sh_sc", [4, 16], F32, kind="Internal", addr_space="Shared").ap()
    x1_spill = nc.dram_tensor("x1_spill", [OWN, D], F32, kind="Internal").ap()
    o_loc = nc.dram_tensor("o_loc", [L, 512], BF16, kind="Internal").ap()
    o_stage = nc.dram_tensor("o_stage", [2 * OWN, 512], BF16, kind="Internal").ap()
    loc_sc = nc.dram_tensor("loc_sc", [1, 32], F32, kind="Internal").ap()

    from contextlib import ExitStack
    with ExitStack() as ctx:
        def sb(name, shape, dt):
            return ctx.enter_context(nc.sbuf_tensor(name, shape, dt))
        def pst(name, shape, dt=F32):
            return ctx.enter_context(nc.psum_tensor(name, shape, dt))
        def sem(name):
            return ctx.enter_context(nc.semaphore(name))

        wbig_h = sb("wbig", [128, 9 * 2048], BF16)
        wbig = wbig_h[:, :]
        wq_sb = wbig[:, 0:4096]
        wk_sb = wbig[:, 4096:8192]
        wvs_sb = wbig[:, 8192:12288]
        w1_sb = wbig[:, 0:18432]
        wb_sb = sb("wb_sb", [128, 8 * 2], BF16)
        wv_sb = sb("wv_sb", [128, 8 * 1024], BF16)
        w2_sb = sb("w2_sb", [128, 16 * 16], BF16)
        wo_sb = sb("wo_sb", [128, 8 * 1024], BF16)

        qkdiag_h = sb("qkdiag", [128, 32 * 128], BF16)
        qdiag = qkdiag_h[:, 0:16 * 128]
        kdiag = qkdiag_h[:, 16 * 128:32 * 128]
        vdiag = qkdiag_h[:, 0:32 * 128]
        sdiag_h = sb("sdiag", [128, 24 * 128], BF16)
        sdiag = sdiag_h[:, :]
        vsdiag = sdiag_h[:, 0:16 * 128]
        firl_sb = sb("firl_sb", [128, 8 * 63], F32)
        firs_sb = sb("firs_sb", [128, 8 * 3], F32)
        cw_sb = sb("cw_sb", [128, 64], F32)
        cw2_sb = sb("cw2_sb", [128, 16], F32)

        ident_b = sb("ident_b", [128, 128], BF16)
        ident_f = sb("ident_f", [128, 128], F32)
        ones_b = sb("ones_b", [128, 128], BF16)
        mask_low = sb("mask_low", [C, C], BF16)
        mask_upc = sb("mask_upc", [C, C], F32)
        gb1_sb = sb("gb1_sb", [128, 16], F32)
        bb_sb = sb("bb_sb", [16, 1], F32)
        itemp_sb = sb("itemp_sb", [128, 64], F32)
        floor_sb = sb("floor_sb", [128, 64], F32)
        fbias_sb = sb("fbias_sb", [128, 1024], BF16)
        onw_sb = sb("onw_sb", [128, 8], F32)
        ws2_bc = sb("ws2_bc", [128, 1], F32)
        sc_bc = sb("sc_bc", [128, 2], F32)
        sA = sb("sA", [128, 128], F32)
        sB = sb("sB", [128, 128], F32)
        sC = sb("sC", [128, 128], F32)
        nonce_sb = sb("nonce_sb", [1, 2], I32)
        me_sb = sb("me_sb", [1, 1], I32)
        flag_sb = sb("flag_sb", [1, 16], I32)

        xt_f = sb("xt_f", [128, D], F32)
        xt_b = sb("xt_b", [128, D], BF16)
        xfm = sb("xfm", [128, 8 * GRP], BF16)
        sbig1_h = sb("sbig1", [128, 8 * (GRP + NH)], BF16)
        wf8_sb = sbig1_h[:, 0:4096].bitcast(FP8)
        qpre = sbig1_h[:, 0:4 * (GRP + NH)]
        kpre = sbig1_h[:, 4 * (GRP + NH):8 * (GRP + NH)]
        vop = sbig1_h[:, 0:8 * (GRP + NH)]
        sbig2_h = sb("sbig2", [128, 8 * (GRP + 64)], BF16)
        vspre = sbig2_h[:, 0:4 * (GRP + NH)]
        qfm = sbig2_h[:, 4 * (GRP + NH):4 * (GRP + NH) + 4 * GRP]
        vow = sbig2_h[:, 0:8 * (GRP + 64)]
        sbig3_h = sb("sbig3", [128, 8 * GRP], BF16)
        kfm = sbig3_h[:, 0:4 * GRP]
        vsfm = sbig3_h[:, 4 * GRP:8 * GRP]
        pth_s_alias = sbig3_h[:, 0:4 * D]
        tstage = sb("tstage", [128, 2 * GRP], F32)
        beta_g = sb("beta_g", [2, GRP], F32)

        q_r = sb("q_r", [128, 4 * C], F32R)
        qsq = sb("qsq", [128, 4 * C], BF16)
        kn_tm = sb("kn_tm", [C, 2 * DV], F32R)
        kn_fm = sb("kn_fm", [128, 4 * C], BF16)
        u_tm = sb("u_tm", [C, 2 * DV], F32R)
        w_fm = sb("w_fm", [128, 4 * C], F32R)
        ui_tm = sb("ui_tm", [C, 2 * DV], F32R)
        attnT = sb("attnT", [C, 2 * C], F32R)
        Lt = sb("Lt", [C, 2 * C], BF16)
        LtT = sb("LtT", [C, 2 * C], BF16)
        Xp = sb("Xp", [C, 2 * C], BF16)
        XpT = sb("XpT", [C, 2 * C], BF16)
        PT = sb("PT", [C, 2 * C], BF16)
        PT2 = sb("PT2", [C, 2 * C], BF16)
        PTr = sb("PTr", [C, 2 * C], F32R)
        beta_tm = sb("beta_tm", [C, 4], F32)
        rs_sc = sb("rs_sc", [128, 16], F32)
        rs_a = sb("rs_a", [128, 16], F32)
        rs_b = sb("rs_b", [128, 16], F32)
        rs_i = sb("rs_i", [128, 16], I32)
        o_tile = sb("o_tile", [C, 512], BF16)

        fir_a = sb("fir_a", [128, GRP], F32)
        logit_f = fir_a[0:16, 0:GRP]
        fir_b = sb("fir_b", [128, GRP], F32)
        fir_o = sb("fir_o", [128, GRP], BF16)
        pth_v = sb("pth_v", [128, 4 * D], BF16)
        pth_s = pth_s_alias
        pth_l = sb("pth_l", [128, 4 * D], BF16)
        pth_d = sb("pth_d", [128, 4 * D], BF16)
        bvbk = sb("bvbk", [C, 2 * 512], F32R)
        S_sb = sb("S_sb", [128, 2 * 512], F32R)
        wstage = pth_d[:, 0:4096].bitcast(F32)
        sq_t = tstage[:, 0:D]
        stat_tm = sb("stat_tm", [128, 4 * 32], F32)
        stat_fm = sb("stat_fm", [32, 4 * 128], BF16)
        hmid = sb("hmid", [128, 2 * GRP], BF16)
        logit_t = sb("logit_t", [128, 64], F32)
        prob_t = sb("prob_t", [128, 64], F32)
        sm_a = sb("sm_a", [128, 64], F32)
        sm_b = sb("sm_b", [128, 96], F32)
        firb2 = None
        comb_a = sb("comb_a", [128, D], BF16)
        comb_b = sb("comb_b", [128, D], BF16)
        orms = sb("orms", [128, D], BF16)
        ofm = sb("ofm", [128, 8 * 128], BF16)
        y_t = sb("y_t", [128, D], F32)
        x1_t = sb("x1_t", [128, D], F32)
        k_tm = x1_t[0:C, 0:2 * DV]
        v_tm = x1_t[0:C, 2 * DV:4 * DV]
        ksq = y_t[0:C, 0:2 * DV]
        ln_sc = sb("ln_sc", [128, 16], F32)
        amax_run = sb("amax_run", [128, 4], F32)
        xs_b = orms
        xs_fmb = ofm
        xs_fm8 = sb("xs_fm8", [128, 8 * 128], FP8)
        ffn_t = comb_a
        out_t = x1_t

        pA = pst("pA", [128, 512])
        pB = pst("pB", [128, 512])
        pC_ = pst("pC", [128, 512])
        pD_ = pst("pD", [128, 512])
        pS0 = pst("pS0", [128, 512])
        pS1 = pst("pS1", [128, 512])
        pT_b = pst("pT_b", [128, 128], BF16)
        pT_f = pst("pT_f", [128, 128], F32)

        s_ld = sem("s_ld")
        s_ld2 = sem("s_ld2")
        s_t = sem("s_t")
        s_v = sem("s_v")
        s_a = sem("s_a")
        s_g = sem("s_g")
        s_st = sem("s_st")
        s_p = sem("s_p")
        s_gd = sem("s_gd")
        s_dbg = sem("s_dbg")

        ctx.enter_context(nc.allow_non_contiguous_dma(reason="tiny strided preamble loads"))
        block = ctx.enter_context(nc.Block())

        state = {"cur": None, "eng": None, "cnt": {}, "rv": {}}

        def I(name, fn, *incs):
            for s_, amt in incs:
                state["cnt"][s_.name] = state["cnt"].get(s_.name, 0) + amt
            if state["cur"] == name:
                ins = fn(state["eng"])
                for s_, amt in incs:
                    ins.then_inc(s_, amt)

        def W(name, s_):
            v = state["cnt"].get(s_.name, 0)
            if state["cur"] == name and v > 0:
                state["eng"].wait_ge(s_, v)

        def DR(name):
            if state["cur"] == name:
                state["eng"].drain()

        def emit(pass_name, eng):
            state["cur"], state["eng"], state["cnt"] = pass_name, eng, {}
            sy = "sync"; te = "tensor"; ve = "vector"; sc = "scalar"; gp = "gpsimd"

            def ld(dst, src, s_=s_ld, q=sy):
                I(q, lambda e: e.dma_start(out=dst, in_=src), (s_, 16))

            def ld2(dst, src):
                I(sc, lambda e: e.dma_start(out=dst, in_=src), (s_ld2, 16))

            def dmat(dst, src):
                I(sc, lambda e: e.dma_start_transpose(out=dst, in_=src), (s_ld2, 16))

            def vop_(fn, *incs):
                I(ve, fn, *(incs or ((s_v, 1),)))

            ld(nonce_sb[:], nonce[:])
            ld(me_sb[:], meid[:])

            I(gp, lambda e: e.memset(ident_b[:], 0.0), (s_g, 1))
            DR(gp)
            I(gp, lambda e: e.affine_select(out=ident_b[:], in_=ident_b[:],
                                            compare_op=ALU.not_equal, fill=1.0, base=0,
                                            pattern=[[-1, 128]], channel_multiplier=1), (s_g, 1))
            I(gp, lambda e: e.memset(ident_f[:], 0.0), (s_g, 1))
            DR(gp)
            I(gp, lambda e: e.affine_select(out=ident_f[:], in_=ident_f[:],
                                            compare_op=ALU.not_equal, fill=1.0, base=0,
                                            pattern=[[-1, 128]], channel_multiplier=1), (s_g, 1))
            I(gp, lambda e: e.memset(ones_b[:], 1.0), (s_g, 1))
            I(gp, lambda e: e.memset(mask_low[:], 1.0), (s_g, 1))
            DR(gp)
            I(gp, lambda e: e.affine_select(out=mask_low[:], in_=mask_low[:],
                                            compare_op=ALU.is_gt, fill=0.0, base=0,
                                            pattern=[[-1, C]], channel_multiplier=1), (s_g, 1))
            I(gp, lambda e: e.memset(mask_upc[:], 1.0), (s_g, 1))
            DR(gp)
            I(gp, lambda e: e.affine_select(out=mask_upc[:], in_=mask_upc[:],
                                            compare_op=ALU.is_ge, fill=0.0, base=0,
                                            pattern=[[1, C]], channel_multiplier=-1), (s_g, 1))

            def load_w(dst, src, nkk, ncols, kk0=0):
                for kk in range(nkk):
                    ld(wstage[:, 0:ncols], src[kk * 128:(kk + 1) * 128, :])
                    W(ve, s_ld)
                    vop_(lambda e, kk=kk: e.tensor_copy(
                        dst[:, (kk0 + kk) * ncols:(kk0 + kk + 1) * ncols], wstage[:, 0:ncols]))
                    W(sy, s_v)

            load_w(wq_sb, wqT, 8, 512)
            load_w(wk_sb, wkT, 8, 512)
            load_w(wb_sb, wbT, 8, 2)
            load_w(wv_sb, wvT, 8, 1024)
            load_w(wvs_sb, wvsT, 8, 512)
            load_w(w2_sb, w2T, 16, 16)

            ld(onw_sb[:, 0:8], onw4[:])
            for kk in range(8):
                ld(wstage[:, 0:1024], woT[kk * 128:(kk + 1) * 128, :])
                W(ve, s_ld)
                vop_(lambda e, kk=kk: e.tensor_scalar(
                    wo_sb[:, kk * 1024:(kk + 1) * 1024], wstage[:, 0:1024],
                    onw_sb[:, kk:kk + 1], None, ALU.mult))
                W(sy, s_v)

            for kk in range(8):
                ld(wstage[:, 0:1024], wfT[kk * 128:(kk + 1) * 128, :])
                W(ve, s_ld)
                if kk == 0:
                    vop_(lambda e: e.tensor_reduce(sA[:, 0:1], wstage[:, 0:1024], AX.X,
                                                   ALU.max, apply_absolute_value=True))
                else:
                    vop_(lambda e: e.tensor_reduce(sA[:, 1:2], wstage[:, 0:1024], AX.X,
                                                   ALU.max, apply_absolute_value=True))
                    DR(ve)
                    vop_(lambda e: e.tensor_tensor(sA[:, 2:3], sA[:, 0:1], sA[:, 1:2], ALU.max))
                    DR(ve)
                    vop_(lambda e: e.tensor_copy(sA[:, 0:1], sA[:, 2:3]))
                    DR(ve)
                W(sy, s_v)
            W(te, s_v)
            W(te, s_g)
            I(te, lambda e: e.transpose(pT_f[0:1, 0:128], sA[:, 0:1], ident_f[:]), (s_t, 1))
            W(ve, s_t)
            vop_(lambda e: e.tensor_reduce(sA[0:1, 3:4], pT_f[0:1, 0:128], AX.X, ALU.max))
            DR(ve)
            vop_(lambda e: e.tensor_scalar(sA[0:1, 4:5], sA[0:1, 3:4], 1e-12, None, ALU.max))
            DR(ve)
            vop_(lambda e: e.reciprocal(sA[0:1, 5:6], sA[0:1, 4:5]))
            DR(ve)
            vop_(lambda e: e.tensor_scalar(sA[0:1, 6:7], sA[0:1, 5:6], 448.0, 448.0, ALU.mult, ALU.min))
            DR(ve)
            vop_(lambda e: e.reciprocal(sA[0:1, 7:8], sA[0:1, 6:7]))
            DR(ve)
            vop_(lambda e: e.tensor_scalar(sA[0:1, 8:9], sA[0:1, 6:7], 0.5, None, ALU.mult))
            DR(ve)
            W(sy, s_v)
            ld(loc_sc[0:1, 0:9], sA[0:1, 0:9])
            W(sy, s_ld)
            ld(ws2_bc[:], loc_sc[0:1, 8:9].broadcast_to((128, 1)))

            ld(gb1_sb[:, 0:16], gb1[:])
            ld(bb_sb[:, 0:1], bb[:])
            I("gpsimd", lambda e: e.dma_start(out=fbias_sb[:], in_=fbias[None, :].broadcast_to((128, 1024))), (s_gd, 16))
            ld(sA[0:1, 16:20], ltemp[:])
            ld(sA[0:1, 20:36], fraw[:])
            ld(cw_sb[:, 0:16], qcw[:])
            ld(cw_sb[:, 16:32], kcw[:])
            ld(cw_sb[:, 32:64], vcw[:])
            ld(cw2_sb[:, 0:16], vscw[:])
            ld(firs_sb[:, 0:24], firs[:])
            ld(firl_sb[:, 0:504], firl[:])

            W(sc, s_ld)
            I(sc, lambda e: e.activation(sB[0:1, 0:4], sA[0:1, 16:20], AF.Abs), (s_a, 1))
            DR(sc)
            I(sc, lambda e: e.activation(sB[0:1, 4:8], sB[0:1, 0:4], AF.Exp, scale=-1.0), (s_a, 1))
            DR(sc)
            I(sc, lambda e: e.activation(sB[0:1, 8:12], sB[0:1, 4:8], AF.Ln, bias=1.0), (s_a, 1))
            DR(sc)
            W(ve, s_a)
            vop_(lambda e: e.tensor_scalar(sB[0:1, 12:16], sA[0:1, 16:20], 0.0, None, ALU.max))
            DR(ve)
            vop_(lambda e: e.tensor_tensor(sB[0:1, 16:20], sB[0:1, 8:12], sB[0:1, 12:16], ALU.add))
            DR(ve)
            vop_(lambda e: e.tensor_scalar(sB[0:1, 20:24], sB[0:1, 16:20], 1e-4, None, ALU.add))
            DR(ve)
            vop_(lambda e: e.reciprocal(sB[0:1, 24:28], sB[0:1, 20:24]))
            DR(ve)
            W(sc, s_v)
            I(sc, lambda e: e.activation(sB[0:1, 32:48], sA[0:1, 20:36], AF.Exp, scale=-1.0), (s_a, 1))
            DR(sc)
            W(ve, s_a)
            vop_(lambda e: e.tensor_scalar(sB[0:1, 48:64], sB[0:1, 32:48], 1.0, None, ALU.add))
            DR(ve)
            vop_(lambda e: e.reciprocal(sB[0:1, 64:80], sB[0:1, 48:64]))
            DR(ve)
            vop_(lambda e: e.tensor_scalar(sB[0:1, 80:96], sB[0:1, 64:80], 0.05, None, ALU.mult))
            DR(ve)
            for hh in range(4):
                for pp_ in range(4):
                    vop_(lambda e, hh=hh, pp_=pp_: e.tensor_copy(
                        sB[0:1, 96 + hh * 4 + pp_: 97 + hh * 4 + pp_], sB[0:1, 24 + hh:25 + hh]))
            DR(ve)
            W(sy, s_v)
            ld(loc_sc[0:1, 0:16], sB[0:1, 96:112])
            W(sy, s_ld)
            ld(itemp_sb[:, 0:16], loc_sc[0:1, 0:16].broadcast_to((128, 16)))
            ld(loc_sc[0:1, 16:32], sB[0:1, 80:96])
            W(sy, s_ld)
            ld(floor_sb[:, 0:16], loc_sc[0:1, 16:32].broadcast_to((128, 16)))
            W(ve, s_ld)
            for r in range(1, 4):
                vop_(lambda e, r=r: e.tensor_copy(itemp_sb[:, r * 16:(r + 1) * 16], itemp_sb[:, 0:16]))
                vop_(lambda e, r=r: e.tensor_copy(floor_sb[:, r * 16:(r + 1) * 16], floor_sb[:, 0:16]))

            def build_diags(dst, wsrc, col0, n_rb, n_tap):
                W(ve, s_g)
                for rb in range(n_rb):
                    for t in range(n_tap):
                        cc0 = (rb * n_tap + t) * 128
                        vop_(lambda e, cc0=cc0, rb=rb, t=t: e.tensor_scalar(
                            dst[:, cc0:cc0 + 128], ones_b[:, 0:128],
                            wsrc[:, col0 + rb * n_tap + t: col0 + rb * n_tap + t + 1],
                            None, ALU.mult))
                W(gp, s_v)
                n = n_rb * n_tap
                I(gp, lambda e: e.affine_select(out=dst[:, 0:n * 128], in_=dst[:, 0:n * 128],
                                                compare_op=ALU.is_equal, fill=0.0, base=0,
                                                pattern=[[0, n], [-1, 128]], channel_multiplier=1),
                  (s_g, 1))

            W(ve, s_ld)
            build_diags(qdiag, cw_sb, 0, 4, 4)
            build_diags(kdiag, cw_sb, 16, 4, 4)
            build_diags(vsdiag, cw2_sb, 0, 4, 4)

            for t_sb in (qpre, kpre, vspre):
                for rb in range(4):
                    vop_(lambda e, t_sb=t_sb, rb=rb: e.memset(
                        t_sb[:, rb * (GRP + NH):rb * (GRP + NH) + NH], 0.0))
            vop_(lambda e: e.memset(amax_run[:], 0.0))

            if state["cur"] == sy:
                W(sy, s_ld)
                state["rv"]["noncer"] = state["eng"].alloc_register("noncer")
                state["eng"].reg_load(state["rv"]["noncer"], nonce_sb[0:1, 0:1])
                state["rv"]["pollr"] = state["eng"].alloc_register("pollr")
            W(sy, s_g)
            W(sy, s_v)
            W(sy, s_a)
            W(te, s_g)
            W(ve, s_g)
            W(sc, s_g)
            W(gp, s_g)

            if state["cur"] == sy:
                me2 = nc.partition_id() % 2
                pp2 = (nc.partition_id() + 1) % 2
                V = state["rv"]
                V["o_wr"] = sh_o[ts(me2, L), :]
                V["o_rd0"] = sh_o[0:L, :][ts(me2, OWN), :]
                V["o_rd1"] = sh_o[L:2 * L, :][ts(me2, OWN), :]
                V["fl_wr"] = sh_flag[ts(me2, 1), :]
                V["fl_rd_all"] = sh_flag[ts(pp2, 1), :]
                V["flam_wr"] = sh_flag[ts(me2 + 2, 1), :]
                V["flam_rd"] = sh_flag[ts(pp2 + 2, 1), :]
                V["sc_wr"] = sh_sc[ts(me2, 1), :]
                V["sc_rd"] = sh_sc[ts(pp2, 1), :]

            for g in range(NG_LIM):
                t0 = g * GRP
                for ti in range(4):
                    W(sy, s_v)
                    ld(xt_f[:], x_seq[t0 + ti * 128: t0 + (ti + 1) * 128, :])
                    W(ve, s_ld)
                    vop_(lambda e: e.tensor_copy(xt_b[:, 0:D], xt_f[:]))
                    W(sc, s_v)
                    W(sc, s_t)
                    for kk in range(8):
                        dmat(xfm[:, kk * GRP + ti * 128: kk * GRP + (ti + 1) * 128],
                             xt_b[:, kk * 128:(kk + 1) * 128])
                    W(ve, s_ld2)
                W(te, s_ld2)

                def proj(pre, w_sb, n_rb, ncols):
                    for rb in range(n_rb):
                        pp = pA if (rb % 2 == 0) else pB
                        for kk in range(8):
                            I(te, lambda e, pp=pp, kk=kk, rb=rb: e.matmul(
                                pp[:, 0:GRP],
                                w_sb[:, kk * ncols + rb * 128: kk * ncols + rb * 128 + 128],
                                xfm[:, kk * GRP:(kk + 1) * GRP],
                                start=(kk == 0), stop=(kk == 7)), (s_t, 1))
                        W(ve, s_t)
                        base = rb * (GRP + NH)
                        if g == 0:
                            vop_(lambda e, pre=pre, base=base: e.memset(
                                pre[:, base:base + NH], 0.0))
                        else:
                            vop_(lambda e, pre=pre, base=base: e.tensor_copy(
                                pre[:, base:base + NH], pre[:, base + GRP:base + GRP + NH]))
                        DR(ve)
                        vop_(lambda e, pre=pre, base=base, pp=pp: e.tensor_copy(
                            pre[:, base + NH:base + NH + GRP], pp[:, 0:GRP]))
                        W(te, s_v)

                proj(qpre, wq_sb, 4, 512)
                proj(kpre, wk_sb, 4, 512)
                proj(vspre, wvs_sb, 4, 512)
                for kk in range(8):
                    I(te, lambda e, kk=kk: e.matmul(
                        pC_[0:2, 0:GRP], wb_sb[:, kk * 2:kk * 2 + 2],
                        xfm[:, kk * GRP:(kk + 1) * GRP],
                        start=(kk == 0), stop=(kk == 7)), (s_t, 1))
                W(sc, s_t)
                I(sc, lambda e: e.activation(tstage[0:2, 0:GRP], pC_[0:2, 0:GRP],
                                             AF.Tanh, scale=0.5), (s_a, 1))
                W(ve, s_a)
                W(ve, s_t)
                vop_(lambda e, t0=t0: e.tensor_scalar(beta_g[0:2, 0:GRP],
                                                      tstage[0:2, 0:GRP], 0.5, 0.5,
                                                      ALU.mult, ALU.add))
                W(te, s_v)

                def conv_silu(pre, dfm, diag, n_rb):
                    for rb in range(n_rb):
                        ppc = pC_ if (rb % 2 == 0) else pD_
                        base = rb * (GRP + NH)
                        for t in range(4):
                            I(te, lambda e, ppc=ppc, diag=diag, rb=rb, t=t, base=base, pre=pre:
                              e.matmul(ppc[:, 0:GRP],
                                       diag[:, (rb * 4 + t) * 128:(rb * 4 + t) * 128 + 128],
                                       pre[:, base + 1 + t: base + 1 + t + GRP],
                                       start=(t == 0), stop=(t == 3)), (s_t, 1))
                        W(sc, s_t)
                        I(sc, lambda e, ppc=ppc: e.activation(tstage[:, 0:GRP], ppc[:, 0:GRP],
                                                              AF.Tanh, scale=0.5), (s_a, 1))
                        W(ve, s_a)
                        vop_(lambda e: e.tensor_scalar(tstage[:, GRP:2 * GRP], tstage[:, 0:GRP],
                                                       0.5, 0.5, ALU.mult, ALU.add))
                        DR(ve)
                        W(ve, s_t)
                        vop_(lambda e, dfm=dfm, rb=rb, ppc=ppc: e.tensor_tensor(
                            dfm[:, rb * GRP:(rb + 1) * GRP], tstage[:, GRP:2 * GRP],
                            ppc[:, 0:GRP], ALU.mult))
                        W(te, s_v)

                conv_silu(qpre, qfm, qdiag, 4)
                conv_silu(kpre, kfm, kdiag, 4)
                conv_silu(vspre, vsfm, vsdiag, 4)

                if DEBUG and g == 0:
                    W(gp, s_v)
                    W(gp, s_a)
                    W(gp, s_ld2)
                    I(gp, lambda e: e.dma_start(out=dbg["xfm"][:], in_=xfm[:, 0:8 * GRP]), (s_dbg, 16))
                    I(gp, lambda e: e.dma_start(out=dbg["qpre"][:], in_=qpre[:, 0:GRP + NH]), (s_dbg, 16))
                    for rb in range(4):
                        I(gp, lambda e, rb=rb: e.dma_start(
                            out=dbg["q_fm"][rb * 128:(rb + 1) * 128, :],
                            in_=qfm[:, rb * GRP:(rb + 1) * GRP]), (s_st, 16))
                    W(sy, s_dbg)
                    W(ve, s_dbg)
                    W(sc, s_dbg)
                    W(te, s_dbg)
                    W(gp, s_dbg)

                for jj in range(GRP // C):
                    j = g * (GRP // C) + jj
                    cc = jj * C
                    W(te, s_v)
                    I(te, lambda e, cc=cc: e.transpose(
                        pT_f[0:C, 0:2], beta_g[0:2, cc:cc + C],
                        ident_f[0:2, 0:2]), (s_t, 1))
                    W(ve, s_t)
                    vop_(lambda e: e.tensor_copy(beta_tm[:, 0:2], pT_f[0:C, 0:2]))
                    for rbk in range(4):
                        W(te, s_v)
                        I(te, lambda e, rbk=rbk, cc=cc: e.transpose(
                            pT_b[0:C, 0:128], kfm[:, rbk * GRP + cc: rbk * GRP + cc + C],
                            ident_b[:]), (s_t, 1))
                        W(ve, s_t)
                        vop_(lambda e, rbk=rbk: e.tensor_copy(
                            k_tm[:, rbk * 128:(rbk + 1) * 128], pT_b[0:C, 0:128]))
                        W(te, s_v)
                        I(te, lambda e, rbk=rbk, cc=cc: e.transpose(
                            pT_b[0:C, 0:128], vsfm[:, rbk * GRP + cc: rbk * GRP + cc + C],
                            ident_b[:]), (s_t, 1))
                        W(ve, s_t)
                        vop_(lambda e, rbk=rbk: e.tensor_copy(
                            v_tm[:, rbk * 128:(rbk + 1) * 128], pT_b[0:C, 0:128]))
                        vop_(lambda e, rbk=rbk, cc=cc: e.tensor_copy(
                            q_r[:, rbk * C:(rbk + 1) * C],
                            qfm[:, rbk * GRP + cc: rbk * GRP + cc + C]))
                        vop_(lambda e, rbk=rbk, cc=cc: e.tensor_tensor(
                            qsq[:, rbk * C:(rbk + 1) * C],
                            qfm[:, rbk * GRP + cc: rbk * GRP + cc + C],
                            qfm[:, rbk * GRP + cc: rbk * GRP + cc + C], ALU.mult))
                    DR(ve)
                    W(te, s_v)
                    for hh in range(2):
                        for sub in range(2):
                            I(te, lambda e, hh=hh, sub=sub: e.matmul(
                                pC_[0:C, 0:1], qsq[:, (hh * 2 + sub) * C:(hh * 2 + sub + 1) * C],
                                ones_b[:, 0:1], start=(sub == 0), stop=(sub == 1)), (s_t, 1))
                        W(ve, s_t)
                        vop_(lambda e, hh=hh: e.tensor_copy(rs_sc[0:C, hh:hh + 1], pC_[0:C, 0:1]))
                        W(te, s_v)
                    vop_(lambda e: e.tensor_tensor(ksq[:, 0:2 * DV], k_tm[:, 0:2 * DV],
                                                   k_tm[:, 0:2 * DV], ALU.mult))
                    DR(ve)
                    vop_(lambda e: e.tensor_reduce(
                        rs_sc[0:C, 2:4], ksq[:, 0:2 * DV].rearrange("p (h dv) -> p h dv", h=2),
                        AX.X, ALU.add))
                    DR(ve)
                    vop_(lambda e: e.tensor_scalar(rs_a[0:C, 0:4], rs_sc[0:C, 0:4], 1e-6, None, ALU.add))
                    DR(ve)
                    vop_(lambda e: e.tensor_scalar(rs_i[0:C, 0:4], rs_a[0:C, 0:4].bitcast(I32),
                                                   10, None, ALU.logical_shift_right))
                    DR(ve)
                    vop_(lambda e: e.tensor_copy(rs_b[0:C, 0:4], rs_i[0:C, 0:4]))
                    DR(ve)
                    vop_(lambda e: e.tensor_scalar(rs_sc[0:C, 4:8], rs_b[0:C, 0:4], MAGIC_HI, -1.0,
                                                   ALU.subtract, ALU.mult))
                    DR(ve)
                    vop_(lambda e: e.tensor_copy(rs_i[0:C, 4:8], rs_sc[0:C, 4:8]))
                    DR(ve)
                    vop_(lambda e: e.tensor_scalar(rs_i[0:C, 8:12], rs_i[0:C, 4:8], 9, None,
                                                   ALU.logical_shift_left))
                    DR(ve)
                    vop_(lambda e: e.tensor_copy(rs_b[0:C, 4:8], rs_i[0:C, 8:12].bitcast(F32)))
                    DR(ve)
                    for it in range(3):
                        vop_(lambda e: e.tensor_tensor(rs_b[0:C, 8:12], rs_b[0:C, 4:8],
                                                       rs_b[0:C, 4:8], ALU.mult))
                        DR(ve)
                        vop_(lambda e: e.tensor_tensor(rs_b[0:C, 12:16], rs_b[0:C, 8:12],
                                                       rs_a[0:C, 0:4], ALU.mult))
                        DR(ve)
                        vop_(lambda e: e.tensor_scalar(rs_sc[0:C, 8:12], rs_b[0:C, 12:16],
                                                       -0.5, 1.5, ALU.mult, ALU.add))
                        DR(ve)
                        vop_(lambda e: e.tensor_tensor(rs_sc[0:C, 12:16], rs_b[0:C, 4:8],
                                                       rs_sc[0:C, 8:12], ALU.mult))
                        DR(ve)
                        vop_(lambda e: e.tensor_copy(rs_b[0:C, 4:8], rs_sc[0:C, 12:16]))
                        DR(ve)
                    for hh in range(2):
                        vop_(lambda e, hh=hh: e.tensor_scalar(
                            kn_tm[:, hh * DV:(hh + 1) * DV], k_tm[:, hh * DV:(hh + 1) * DV],
                            rs_b[0:C, 6 + hh:7 + hh], None, ALU.mult))
                        vop_(lambda e, hh=hh: e.tensor_scalar(
                            bvbk[:, hh * 512:hh * 512 + DV], v_tm[:, hh * DV:(hh + 1) * DV],
                            beta_tm[:, hh:hh + 1], None, ALU.mult))
                    DR(ve)
                    for hh in range(2):
                        vop_(lambda e, hh=hh: e.tensor_scalar(
                            bvbk[:, hh * 512 + DV:hh * 512 + 512],
                            kn_tm[:, hh * DV:(hh + 1) * DV],
                            beta_tm[:, hh:hh + 1], None, ALU.mult))
                    W(te, s_v)
                    for rbk in range(4):
                        I(te, lambda e, rbk=rbk: e.transpose(
                            pT_f[0:128, 0:C],
                            kn_tm[:, rbk * 128:(rbk + 1) * 128].bitcast(F32),
                            ident_f[0:C, 0:C]), (s_t, 1))
                        W(ve, s_t)
                        vop_(lambda e, rbk=rbk: e.tensor_copy(
                            kn_fm[:, rbk * C:(rbk + 1) * C], pT_f[0:128, 0:C]))
                        W(te, s_v)
                    if DEBUG and j == 0:
                        W(sy, s_v)
                        I(sy, lambda e: e.dma_start(out=dbg["kn0"], in_=kn_tm[:].bitcast(F32)), (s_dbg, 16))
                    W(sy, s_dbg)
                    W(ve, s_dbg)
                    W(sc, s_dbg)
                    W(te, s_dbg)
                    W(gp, s_dbg)

                    for hh in range(2):
                        for sub in range(2):
                            I(te, lambda e, hh=hh, sub=sub: e.matmul(
                                pC_[0:C, 0:C],
                                kn_fm[:, (hh * 2 + sub) * C:(hh * 2 + sub + 1) * C],
                                kn_fm[:, (hh * 2 + sub) * C:(hh * 2 + sub + 1) * C],
                                start=(sub == 0), stop=(sub == 1)), (s_t, 1))
                        W(ve, s_t)
                        vop_(lambda e, hh=hh: e.tensor_scalar(rs_a[0:C, 8 + hh:9 + hh],
                                                              beta_tm[:, hh:hh + 1], -1.0,
                                                              None, ALU.mult))
                        DR(ve)
                        vop_(lambda e, hh=hh: e.scalar_tensor_tensor(
                            Lt[:, hh * C:(hh + 1) * C], pC_[0:C, 0:C],
                            rs_a[0:C, 8 + hh:9 + hh], mask_low[:], ALU.mult, ALU.mult))
                        W(te, s_v)
                        I(te, lambda e, hh=hh: e.transpose(
                            pT_b[0:C, 0:C], Lt[:, hh * C:(hh + 1) * C], ident_b[0:C, 0:C]), (s_t, 1))
                        W(ve, s_t)
                        vop_(lambda e, hh=hh: e.tensor_copy(LtT[:, hh * C:(hh + 1) * C], pT_b[0:C, 0:C]))
                        DR(ve)
                        vop_(lambda e, hh=hh: e.tensor_tensor(
                            PT[:, hh * C:(hh + 1) * C], ident_b[0:C, 0:C],
                            LtT[:, hh * C:(hh + 1) * C], ALU.add))
                        vop_(lambda e, hh=hh: e.tensor_copy(Xp[:, hh * C:(hh + 1) * C],
                                                            Lt[:, hh * C:(hh + 1) * C]))
                        vop_(lambda e, hh=hh: e.tensor_copy(XpT[:, hh * C:(hh + 1) * C],
                                                            LtT[:, hh * C:(hh + 1) * C]))
                        W(te, s_v)
                        for lev in range(5):
                            I(te, lambda e, hh=hh: e.matmul(
                                pC_[0:C, 0:C], XpT[:, hh * C:(hh + 1) * C],
                                Xp[:, hh * C:(hh + 1) * C], start=True, stop=True), (s_t, 1))
                            W(ve, s_t)
                            vop_(lambda e, hh=hh: e.tensor_copy(Xp[:, hh * C:(hh + 1) * C],
                                                                pC_[0:C, 0:C]))
                            W(te, s_v)
                            I(te, lambda e, hh=hh: e.transpose(
                                pT_b[0:C, 0:C], Xp[:, hh * C:(hh + 1) * C], ident_b[0:C, 0:C]), (s_t, 1))
                            I(te, lambda e, hh=hh: e.matmul(
                                pD_[0:C, 0:C], Xp[:, hh * C:(hh + 1) * C],
                                PT[:, hh * C:(hh + 1) * C], start=True, stop=True), (s_t, 1))
                            W(ve, s_t)
                            vop_(lambda e, hh=hh: e.tensor_copy(XpT[:, hh * C:(hh + 1) * C],
                                                                pT_b[0:C, 0:C]))
                            vop_(lambda e, hh=hh: e.tensor_tensor(
                                PT2[:, hh * C:(hh + 1) * C], pD_[0:C, 0:C],
                                PT[:, hh * C:(hh + 1) * C], ALU.add))
                            DR(ve)
                            vop_(lambda e, hh=hh: e.tensor_copy(PT[:, hh * C:(hh + 1) * C],
                                                                PT2[:, hh * C:(hh + 1) * C]))
                            DR(ve)
                            W(te, s_v)
                        vop_(lambda e, hh=hh: e.tensor_copy(PTr[:, hh * C:(hh + 1) * C],
                                                            PT[:, hh * C:(hh + 1) * C]))
                        W(te, s_v)
                        I(te, lambda e, hh=hh: e.matmul(
                            pC_[0:C, 0:DV], PTr[:, hh * C:(hh + 1) * C],
                            bvbk[:, hh * 512:hh * 512 + DV], start=True, stop=True), (s_t, 1))
                        for sub in range(2):
                            I(te, lambda e, hh=hh, sub=sub: e.matmul(
                                pD_[0:128, sub * C:(sub + 1) * C],
                                bvbk[:, hh * 512 + DV + sub * 128: hh * 512 + DV + (sub + 1) * 128],
                                PTr[:, hh * C:(hh + 1) * C], start=True, stop=True), (s_t, 1))
                        W(ve, s_t)
                        vop_(lambda e, hh=hh: e.tensor_copy(u_tm[:, hh * DV:(hh + 1) * DV],
                                                            pC_[0:C, 0:DV]))
                        vop_(lambda e, hh=hh: e.tensor_copy(w_fm[:, (hh * 2) * C:(hh * 2 + 1) * C],
                                                            pD_[0:128, 0:C]))
                        vop_(lambda e, hh=hh: e.tensor_copy(w_fm[:, (hh * 2 + 1) * C:(hh * 2 + 2) * C],
                                                            pD_[0:128, C:2 * C]))
                        W(te, s_v)
                        for sub in range(2):
                            I(te, lambda e, hh=hh, sub=sub, cc=cc: e.matmul(
                                pC_[0:C, 0:C],
                                kn_fm[:, (hh * 2 + sub) * C:(hh * 2 + sub + 1) * C],
                                qfm[:, (hh * 2 + sub) * GRP + cc:(hh * 2 + sub) * GRP + cc + C],
                                start=(sub == 0), stop=(sub == 1)), (s_t, 1))
                        W(ve, s_t)
                        vop_(lambda e, hh=hh: e.tensor_tensor(
                            attnT[:, hh * C:(hh + 1) * C], pC_[0:C, 0:C], mask_upc[:], ALU.mult))
                        W(te, s_v)
                        if j > 0:
                            for sub in range(2):
                                I(te, lambda e, hh=hh, sub=sub: e.matmul(
                                    pC_[0:C, 0:DV],
                                    w_fm[:, (hh * 2 + sub) * C:(hh * 2 + sub + 1) * C],
                                    S_sb[:, hh * 512 + sub * DV: hh * 512 + (sub + 1) * DV],
                                    start=(sub == 0), stop=(sub == 1)), (s_t, 1))
                        W(ve, s_t)
                        DR(ve)
                        if j == 0:
                            vop_(lambda e, hh=hh: e.tensor_copy(
                                ui_tm[:, hh * DV:(hh + 1) * DV], u_tm[:, hh * DV:(hh + 1) * DV]))
                        else:
                            vop_(lambda e, hh=hh: e.tensor_tensor(
                                ui_tm[:, hh * DV:(hh + 1) * DV], u_tm[:, hh * DV:(hh + 1) * DV],
                                pC_[0:C, 0:DV], ALU.subtract))
                        W(te, s_v)
                        if j == 0:
                            I(te, lambda e, hh=hh: e.matmul(
                                pD_[0:C, 0:DV], attnT[:, hh * C:(hh + 1) * C],
                                ui_tm[:, hh * DV:(hh + 1) * DV], start=True, stop=True), (s_t, 1))
                        else:
                            for sub in range(2):
                                I(te, lambda e, hh=hh, sub=sub: e.matmul(
                                    pD_[0:C, 0:DV],
                                    q_r[:, (hh * 2 + sub) * C:(hh * 2 + sub + 1) * C],
                                    S_sb[:, hh * 512 + sub * DV: hh * 512 + (sub + 1) * DV],
                                    start=(sub == 0), stop=False), (s_t, 1))
                            I(te, lambda e, hh=hh: e.matmul(
                                pD_[0:C, 0:DV], attnT[:, hh * C:(hh + 1) * C],
                                ui_tm[:, hh * DV:(hh + 1) * DV], start=False, stop=True), (s_t, 1))
                        pS = pS0 if hh == 0 else pS1
                        for sub in range(2):
                            I(te, lambda e, pS=pS, hh=hh, sub=sub, j=j: e.matmul(
                                pS[:, sub * DV:(sub + 1) * DV],
                                kn_tm[:, hh * DV + sub * 128: hh * DV + (sub + 1) * 128],
                                ui_tm[:, hh * DV:(hh + 1) * DV],
                                start=(j == 0), stop=False,
                                skip_group_check=True), (s_t, 1))
                        W(ve, s_t)
                        W(ve, s_ld)
                        vop_(lambda e, hh=hh: e.tensor_scalar(
                            o_tile[:, hh * DV:(hh + 1) * DV], pD_[0:C, 0:DV],
                            rs_b[0:C, 4 + hh:5 + hh], None, ALU.mult))
                        vop_(lambda e, pS=pS, hh=hh: e.tensor_copy(
                            S_sb[:, hh * 512:hh * 512 + 512], pS[:, 0:512]))
                        W(te, s_v)

                    if DEBUG and j == 0:
                        W(sy, s_v)
                        W(gp, s_v)
                        I(sy, lambda e: e.dma_start(
                            out=dbg["uw0"][:, 0:512], in_=u_tm[:].bitcast(F32)), (s_dbg, 16))
                        I(sy, lambda e: e.dma_start(
                            out=dbg["uw0"][:, 512:1024], in_=ui_tm[:].bitcast(F32)), (s_dbg, 16))
                        I(gp, lambda e: e.dma_start(out=dbg["o0"][:],
                                                    in_=o_tile[:]), (s_dbg, 16))
                    W(sy, s_dbg)
                    W(ve, s_dbg)
                    W(sc, s_dbg)
                    W(te, s_dbg)
                    W(gp, s_dbg)
                    W(sy, s_v)
                    ld(o_loc[j * C:(j + 1) * C, :], o_tile[:])


            W(sy, s_ld)
            for piece in range(4):
                if state["cur"] == sy:
                    e = state["eng"]
                    ins = e.dma_start(out=state["rv"]["o_wr"][piece * 2048:(piece + 1) * 2048, :],
                                      in_=o_loc[piece * 2048:(piece + 1) * 2048, :])
                state["cnt"][s_ld.name] = state["cnt"].get(s_ld.name, 0) + 16
                if state["cur"] == sy:
                    ins.then_inc(s_ld, 16)
            W(sy, s_ld)
            if state["cur"] == sy:
                e = state["eng"]
                ins = e.dma_start(out=state["rv"]["fl_wr"][:, 0:1], in_=nonce_sb[0:1, 0:1])
            state["cnt"][s_ld.name] = state["cnt"].get(s_ld.name, 0) + 16
            if state["cur"] == sy:
                ins.then_inc(s_ld, 16)
                pollr, noncer = state["rv"]["pollr"], state["rv"]["noncer"]
                e.reg_mov(pollr, 1)
                wh = e.While(pollr)
                wh.__enter__()
                e.sem_clear(s_p)
                e.dma_start(out=flag_sb[0:1, 0:1],
                            in_=state["rv"]["fl_rd_all"][0:1, 0:1]).then_inc(s_p, 16)
                e.wait_ge(s_p, 16)
                e.reg_load(pollr, flag_sb[0:1, 0:1])
                e.reg_alu(pollr, pollr, noncer, ALU.not_equal)
                wh.__exit__(None, None, None)
            for r in range(2):
                for piece in range(2):
                    if state["cur"] == sy:
                        e = state["eng"]
                        reg = state["rv"]["o_rd0"] if r == 0 else state["rv"]["o_rd1"]
                        ins = e.dma_start(out=o_stage[r * OWN + piece * 2048: r * OWN + (piece + 1) * 2048, :],
                                          in_=reg[piece * 2048:(piece + 1) * 2048, :])
                    state["cnt"][s_ld.name] = state["cnt"].get(s_ld.name, 0) + 16
                    if state["cur"] == sy:
                        ins.then_inc(s_ld, 16)
            W(sy, s_ld)

            load_w(w1_sb, w1T, 8, 2048)
            ld(wstage[0:32, 0:2048], w1T[1024:1056, :])
            W(ve, s_ld)
            vop_(lambda e: e.tensor_copy(w1_sb[0:32, 8 * 2048:9 * 2048], wstage[0:32, 0:2048]))
            W(sy, s_v)
            build_diags(vdiag, cw_sb, 32, 8, 4)
            build_diags(sdiag, firs_sb, 0, 8, 3)
            W(te, s_g)
            W(te, s_v)

            def xfm_own(r0, n_ti):
                for ti_ in range(n_ti):
                    W(sy, s_v)
                    ld(xt_f[:], x_own[r0 + ti_ * 128: r0 + (ti_ + 1) * 128, :])
                    W(ve, s_ld)
                    vop_(lambda e: e.tensor_copy(xt_b[:, 0:D], xt_f[:]))
                    W(sc, s_v)
                    W(sc, s_t)
                    for kk in range(8):
                        dmat(xfm[:, kk * GRP + ti_ * 128: kk * GRP + (ti_ + 1) * 128],
                             xt_b[:, kk * 128:(kk + 1) * 128])
                    W(ve, s_ld2)
                W(te, s_ld2)

            def v_own(n_tok, halo_mode, first=False):
                for rb in range(8):
                    pp = pA if (rb % 2 == 0) else pB
                    for kk in range(8):
                        I(te, lambda e, pp=pp, rb=rb, kk=kk, n_tok=n_tok: e.matmul(
                            pp[:, 0:n_tok],
                            wv_sb[:, kk * 1024 + rb * 128: kk * 1024 + rb * 128 + 128],
                            xfm[:, kk * GRP: kk * GRP + n_tok],
                            start=(kk == 0), stop=(kk == 7)), (s_t, 1))
                    W(ve, s_t)
                    base = rb * (GRP + NH)
                    if halo_mode:
                        vop_(lambda e, base=base: e.memset(vop[:, base:base + NH], 0.0))
                    elif not first:
                        vop_(lambda e, base=base: e.tensor_copy(
                            vop[:, base:base + NH],
                            vop[:, base + GRP: base + GRP + NH]))
                    DR(ve)
                    vop_(lambda e, base=base, pp=pp, n_tok=n_tok: e.tensor_copy(
                        vop[:, base + NH: base + NH + n_tok], pp[:, 0:n_tok]))
                    W(te, s_v)
                    ppc = pC_ if (rb % 2 == 0) else pD_
                    for t in range(4):
                        I(te, lambda e, ppc=ppc, rb=rb, t=t, base=base, n_tok=n_tok: e.matmul(
                            ppc[:, 0:n_tok],
                            vdiag[:, (rb * 4 + t) * 128:(rb * 4 + t) * 128 + 128],
                            vop[:, base + 1 + t: base + 1 + t + n_tok],
                            start=(t == 0), stop=(t == 3)), (s_t, 1))
                    W(sc, s_t)
                    I(sc, lambda e, ppc=ppc, n_tok=n_tok: e.activation(
                        tstage[:, 0:n_tok], ppc[:, 0:n_tok], AF.Tanh, scale=0.5), (s_a, 1))
                    W(ve, s_a)
                    vop_(lambda e, n_tok=n_tok: e.tensor_scalar(
                        tstage[:, GRP:GRP + n_tok], tstage[:, 0:n_tok], 0.5, 0.5, ALU.mult, ALU.add))
                    DR(ve)
                    W(ve, s_t)
                    vbase = rb * (GRP + 64)
                    if halo_mode:
                        vop_(lambda e, n_tok=n_tok, ppc=ppc: e.tensor_tensor(
                            tstage[:, 2 * GRP - 128:2 * GRP], tstage[:, GRP:GRP + n_tok],
                            ppc[:, 0:n_tok], ALU.mult))
                        DR(ve)
                        vop_(lambda e, vbase=vbase: e.tensor_copy(
                            vow[:, vbase:vbase + 64], tstage[:, 2 * GRP - 64:2 * GRP]))
                        vop_(lambda e, base=base: e.tensor_copy(
                            vop[:, base:base + NH],
                            vop[:, base + NH + 128 - NH: base + NH + 128]))
                    else:
                        vop_(lambda e, vbase=vbase, n_tok=n_tok, ppc=ppc: e.tensor_tensor(
                            vow[:, vbase + 64: vbase + 64 + n_tok], tstage[:, GRP:GRP + n_tok],
                            ppc[:, 0:n_tok], ALU.mult))
                    W(te, s_v)

            xfm_own(0, 1)
            v_own(128, True)

            for og in range(NOG_LIM):
                ot0 = og * GRP
                xfm_own(XHALO + ot0, 4)
                v_own(GRP, False, first=(og == 0))

                W(ve, s_t)
                DR(ve)
                for rb in range(8):
                    vbase = rb * (GRP + 64)
                    vop_(lambda e, vbase=vbase, rb=rb: e.tensor_scalar(
                        fir_a[:], vow[:, vbase + 2: vbase + 2 + GRP],
                        firl_sb[:, rb * 63: rb * 63 + 1], None, ALU.mult))
                    DR(ve)
                    cur, oth = fir_b, fir_a
                    for t in range(1, 63):
                        vop_(lambda e, cur=cur, oth=oth, vbase=vbase, rb=rb, t=t:
                             e.scalar_tensor_tensor(cur[:], vow[:, vbase + 2 + t: vbase + 2 + t + GRP],
                                                    firl_sb[:, rb * 63 + t: rb * 63 + t + 1],
                                                    oth[:], ALU.mult, ALU.add))
                        DR(ve)
                        cur, oth = oth, cur
                    W(ve, s_ld2)
                    DR(ve)
                    vop_(lambda e, oth=oth: e.tensor_copy(fir_o[:], oth[:]))
                    W(sc, s_v)
                    for ti in range(4):
                        dmat(pth_l[:, ti * D + rb * 128: ti * D + (rb + 1) * 128],
                             fir_o[:, ti * 128:(ti + 1) * 128])
                    vop_(lambda e, vbase=vbase, rb=rb: e.tensor_scalar(
                        fir_a[:], vow[:, vbase + 62: vbase + 62 + GRP],
                        firs_sb[:, rb * 3: rb * 3 + 1], None, ALU.mult))
                    DR(ve)
                    vop_(lambda e, vbase=vbase, rb=rb: e.scalar_tensor_tensor(
                        fir_b[:], vow[:, vbase + 63: vbase + 63 + GRP],
                        firs_sb[:, rb * 3 + 1: rb * 3 + 2], fir_a[:], ALU.mult, ALU.add))
                    DR(ve)
                    vop_(lambda e, vbase=vbase, rb=rb: e.scalar_tensor_tensor(
                        fir_a[:], vow[:, vbase + 64: vbase + 64 + GRP],
                        firs_sb[:, rb * 3 + 2: rb * 3 + 3], fir_b[:], ALU.mult, ALU.add))
                    DR(ve)
                    W(ve, s_ld2)
                    DR(ve)
                    vop_(lambda e: e.tensor_copy(fir_o[:], fir_a[:]))
                    W(sc, s_v)
                    for ti in range(4):
                        dmat(pth_s[:, ti * D + rb * 128: ti * D + (rb + 1) * 128],
                             fir_o[:, ti * 128:(ti + 1) * 128])
                        dmat(pth_v[:, ti * D + rb * 128: ti * D + (rb + 1) * 128],
                             vow[:, vbase + 64 + ti * 128: vbase + 64 + (ti + 1) * 128])
                    vop_(lambda e, vbase=vbase: e.tensor_copy(
                        vow[:, vbase:vbase + 64], vow[:, vbase + GRP: vbase + GRP + 64]))

                W(sy, s_v)
                for ti in range(4):
                    for r in range(2):
                        row = r * OWN + og * GRP + ti * 128
                        ld(pth_d[:, ti * D + r * 512: ti * D + r * 512 + 512],
                           o_stage[row:row + 128, :])
                W(ve, s_ld)
                W(ve, s_ld2)
                W(te, s_ld)
                W(te, s_ld2)

                for ti in range(4):
                    tcol = ti * D
                    for pi, pth in enumerate([pth_s, pth_l, pth_d, pth_v]):
                        vop_(lambda e, pth=pth, tcol=tcol, ti=ti, pi=pi: e.tensor_reduce(
                            stat_tm[:, ti * 32 + 8 * pi: ti * 32 + 8 * pi + 8:2],
                            pth[:, tcol:tcol + D].rearrange("p (h dv) -> p h dv", h=4),
                            AX.X, ALU.add))
                        W(sc, s_v)
                        I(sc, lambda e, pth=pth, tcol=tcol: e.activation(
                            sq_t[:, 0:D], pth[:, tcol:tcol + D], AF.Square), (s_a, 1))
                        W(ve, s_a)
                        vop_(lambda e, ti=ti, pi=pi: e.tensor_reduce(
                            stat_tm[:, ti * 32 + 8 * pi + 1: ti * 32 + 8 * pi + 8:2],
                            sq_t[:, 0:D].rearrange("p (h dv) -> p h dv", h=4),
                            AX.X, ALU.add))
                        DR(ve)
                    vop_(lambda e, ti=ti: e.tensor_scalar(sm_a[:, 0:32], stat_tm[:, ti * 32: ti * 32 + 32],
                                                          1.0 / DV, None, ALU.mult))
                    DR(ve)
                    vop_(lambda e: e.tensor_tensor(sm_b[:, 0:32], sm_a[:, 0:32], sm_a[:, 0:32], ALU.mult))
                    DR(ve)
                    vop_(lambda e, ti=ti: e.tensor_tensor(
                        stat_tm[:, ti * 32 + 1: ti * 32 + 32: 2],
                        sm_a[:, 1:32:2], sm_b[:, 0:32:2], ALU.subtract))
                    vop_(lambda e, ti=ti: e.tensor_copy(
                        stat_tm[:, ti * 32: ti * 32 + 32: 2], sm_a[:, 0:32:2]))
                    DR(ve)
                if DEBUG and og == 0:
                    W(sy, s_v)
                    W(gp, s_v)
                    W(gp, s_ld2)
                    I(sy, lambda e: e.dma_start(out=dbg["stat"], in_=stat_tm[:]), (s_dbg, 16))
                    I(gp, lambda e: e.dma_start(out=dbg["pths"][:, 0:D], in_=pth_s[:, 0:D]), (s_dbg, 16))
                    I(gp, lambda e: e.dma_start(out=dbg["pths"][:, D:2 * D], in_=pth_l[:, 0:D]), (s_dbg, 16))
                    I(gp, lambda e: e.dma_start(out=dbg["pths"][:, 2 * D:3 * D], in_=pth_d[:, 0:D]), (s_dbg, 16))
                    I(gp, lambda e: e.dma_start(out=dbg["pths"][:, 3 * D:4 * D], in_=pth_v[:, 0:D]), (s_dbg, 16))
                    W(sy, s_dbg)
                    W(ve, s_dbg)
                    W(sc, s_dbg)
                    W(te, s_dbg)
                    W(gp, s_dbg)

                W(te, s_v)
                for ti in range(4):
                    I(te, lambda e, ti=ti: e.transpose(
                        pT_f[0:32, 0:128], stat_tm[:, ti * 32:(ti + 1) * 32], ident_f[:]), (s_t, 1))
                    W(ve, s_t)
                    vop_(lambda e, ti=ti: e.tensor_copy(
                        stat_fm[0:32, ti * 128:(ti + 1) * 128], pT_f[0:32, 0:128]))
                    W(te, s_v)

                for m in range(16):
                    pp = pA if (m % 2 == 0) else pB
                    for kk in range(8):
                        I(te, lambda e, pp=pp, m=m, kk=kk: e.matmul(
                            pp[:, 0:GRP], w1_sb[:, kk * 2048 + m * 128: kk * 2048 + m * 128 + 128],
                            xfm[:, kk * GRP:(kk + 1) * GRP],
                            start=(kk == 0), stop=False), (s_t, 1))
                    I(te, lambda e, pp=pp, m=m: e.matmul(
                        pp[:, 0:GRP], w1_sb[0:32, 8 * 2048 + m * 128: 8 * 2048 + m * 128 + 128],
                        stat_fm[0:32, 0:GRP], start=False, stop=True), (s_t, 1))
                    W(sc, s_t)
                    I(sc, lambda e, pp=pp, m=m: e.activation(
                        hmid[:, (m % 2) * GRP:(m % 2) * GRP + GRP], pp[:, 0:GRP],
                        AF.Gelu, bias=gb1_sb[:, m:m + 1]), (s_a, 1))
                    W(te, s_a)
                    I(te, lambda e, m=m: e.matmul(
                        pC_[0:16, 0:GRP], w2_sb[:, m * 16:(m + 1) * 16],
                        hmid[:, (m % 2) * GRP:(m % 2) * GRP + GRP],
                        start=(m == 0), stop=(m == 15)), (s_t, 1))
                W(ve, s_t)
                vop_(lambda e: e.tensor_scalar(logit_f[:, 0:GRP], pC_[0:16, 0:GRP],
                                               bb_sb[0:16, 0:1], None, ALU.add))
                W(te, s_v)
                for ti in range(4):
                    I(te, lambda e, ti=ti: e.transpose(
                        pT_f[0:128, 0:16], logit_f[:, ti * 128:(ti + 1) * 128],
                        ident_f[0:16, 0:16]), (s_t, 1))
                    W(ve, s_t)
                    vop_(lambda e, ti=ti: e.tensor_copy(logit_t[:, ti * 16:(ti + 1) * 16],
                                                        pT_f[0:128, 0:16]))
                    W(te, s_v)
                DR(ve)
                vop_(lambda e: e.tensor_tensor(sm_a[:, 0:64], logit_t[:, 0:64],
                                               itemp_sb[:, 0:64], ALU.mult))
                DR(ve)
                vop_(lambda e: e.tensor_reduce(sm_b[:, 0:16],
                                               sm_a[:, 0:64].rearrange("p (t q) -> p t q", q=4),
                                               AX.X, ALU.max))
                DR(ve)
                for th in range(16):
                    vop_(lambda e, th=th: e.tensor_scalar(
                        prob_t[:, th * 4:(th + 1) * 4], sm_a[:, th * 4:(th + 1) * 4],
                        sm_b[:, th:th + 1], None, ALU.subtract))
                W(sc, s_v)
                I(sc, lambda e: e.activation(sm_a[:, 0:64], prob_t[:, 0:64], AF.Tanh, scale=0.5), (s_a, 1))
                W(ve, s_a)
                vop_(lambda e: e.tensor_scalar(sm_b[:, 16:80], sm_a[:, 0:64], -1.0, 1.0,
                                               ALU.mult, ALU.add))
                DR(ve)
                vop_(lambda e: e.reciprocal(prob_t[:, 0:64], sm_b[:, 16:80]))
                DR(ve)
                vop_(lambda e: e.tensor_scalar(sm_b[:, 16:80], sm_a[:, 0:64], 1.0, None, ALU.add))
                DR(ve)
                vop_(lambda e: e.tensor_tensor(sm_a[:, 0:64], sm_b[:, 16:80], prob_t[:, 0:64], ALU.mult))
                DR(ve)
                vop_(lambda e: e.tensor_reduce(sm_b[:, 0:16],
                                               sm_a[:, 0:64].rearrange("p (t q) -> p t q", q=4),
                                               AX.X, ALU.add))
                DR(ve)
                vop_(lambda e: e.reciprocal(sm_b[:, 16:32], sm_b[:, 0:16]))
                DR(ve)
                for th in range(16):
                    vop_(lambda e, th=th: e.tensor_scalar(
                        prob_t[:, th * 4:(th + 1) * 4], sm_a[:, th * 4:(th + 1) * 4],
                        sm_b[:, 16 + th:17 + th], None, ALU.mult))
                DR(ve)
                vop_(lambda e: e.tensor_tensor(sm_a[:, 0:64], prob_t[:, 0:64],
                                               floor_sb[:, 0:64], ALU.max))
                DR(ve)
                vop_(lambda e: e.tensor_reduce(sm_b[:, 0:16],
                                               sm_a[:, 0:64].rearrange("p (t q) -> p t q", q=4),
                                               AX.X, ALU.add))
                DR(ve)
                vop_(lambda e: e.reciprocal(sm_b[:, 16:32], sm_b[:, 0:16]))
                DR(ve)
                for th in range(16):
                    vop_(lambda e, th=th: e.tensor_scalar(
                        prob_t[:, th * 4:(th + 1) * 4], sm_a[:, th * 4:(th + 1) * 4],
                        sm_b[:, 16 + th:17 + th], None, ALU.mult))
                DR(ve)
                if DEBUG and og == 0:
                    W(sy, s_v)
                    I(sy, lambda e: e.dma_start(out=dbg["probs"], in_=prob_t[:]), (s_dbg, 16))
                    W(sy, s_dbg)
                    W(ve, s_dbg)
                    W(sc, s_dbg)
                    W(te, s_dbg)
                    W(gp, s_dbg)

                for ti in range(4):
                    tcol = ti * D
                    for hh in range(4):
                        vop_(lambda e, hh=hh, tcol=tcol, ti=ti: e.tensor_scalar(
                            comb_a[:, hh * DV:(hh + 1) * DV],
                            pth_s[:, tcol + hh * DV: tcol + (hh + 1) * DV],
                            prob_t[:, ti * 16 + hh * 4: ti * 16 + hh * 4 + 1], None, ALU.mult))
                    DR(ve)
                    for hh in range(4):
                        vop_(lambda e, hh=hh, tcol=tcol, ti=ti: e.scalar_tensor_tensor(
                            comb_b[:, hh * DV:(hh + 1) * DV],
                            pth_l[:, tcol + hh * DV: tcol + (hh + 1) * DV],
                            prob_t[:, ti * 16 + hh * 4 + 1: ti * 16 + hh * 4 + 2],
                            comb_a[:, hh * DV:(hh + 1) * DV], ALU.mult, ALU.add))
                    DR(ve)
                    for hh in range(4):
                        vop_(lambda e, hh=hh, tcol=tcol, ti=ti: e.scalar_tensor_tensor(
                            comb_a[:, hh * DV:(hh + 1) * DV],
                            pth_d[:, tcol + hh * DV: tcol + (hh + 1) * DV],
                            prob_t[:, ti * 16 + hh * 4 + 2: ti * 16 + hh * 4 + 3],
                            comb_b[:, hh * DV:(hh + 1) * DV], ALU.mult, ALU.add))
                    DR(ve)
                    for hh in range(4):
                        vop_(lambda e, hh=hh, tcol=tcol, ti=ti: e.scalar_tensor_tensor(
                            comb_b[:, hh * DV:(hh + 1) * DV],
                            pth_v[:, tcol + hh * DV: tcol + (hh + 1) * DV],
                            prob_t[:, ti * 16 + hh * 4 + 3: ti * 16 + hh * 4 + 4],
                            comb_a[:, hh * DV:(hh + 1) * DV], ALU.mult, ALU.add))
                    DR(ve)
                    vop_(lambda e: e.tensor_tensor(sq_t[:, 0:D], comb_b[:, 0:D], comb_b[:, 0:D], ALU.mult))
                    DR(ve)
                    vop_(lambda e: e.tensor_reduce(ln_sc[:, 0:4],
                                                   sq_t[:, 0:D].rearrange("p (h dv) -> p h dv", h=4),
                                                   AX.X, ALU.add))
                    DR(ve)
                    vop_(lambda e: e.tensor_scalar(rs_a[:, 0:4], ln_sc[:, 0:4], 1.0 / DV, 1e-5,
                                                   ALU.mult, ALU.add))
                    DR(ve)

                    def newton4(xcols, ycols, n=4):
                        x0, x1_ = xcols
                        y0, y1_ = ycols
                        vop_(lambda e: e.tensor_scalar(rs_i[:, 0:n], rs_a[:, x0:x1_].bitcast(I32),
                                                       10, None, ALU.logical_shift_right))
                        DR(ve)
                        vop_(lambda e: e.tensor_copy(rs_sc[:, 0:n], rs_i[:, 0:n]))
                        DR(ve)
                        vop_(lambda e: e.tensor_scalar(rs_sc[:, 4:4 + n], rs_sc[:, 0:n], MAGIC_HI,
                                                       -1.0, ALU.subtract, ALU.mult))
                        DR(ve)
                        vop_(lambda e: e.tensor_copy(rs_i[:, 4:4 + n], rs_sc[:, 4:4 + n]))
                        DR(ve)
                        vop_(lambda e: e.tensor_scalar(rs_i[:, 8:8 + n], rs_i[:, 4:4 + n], 9, None,
                                                       ALU.logical_shift_left))
                        DR(ve)
                        vop_(lambda e: e.tensor_copy(rs_b[:, y0:y1_], rs_i[:, 8:8 + n].bitcast(F32)))
                        DR(ve)
                        for it in range(3):
                            vop_(lambda e: e.tensor_tensor(rs_sc[:, 0:n], rs_b[:, y0:y1_],
                                                           rs_b[:, y0:y1_], ALU.mult))
                            DR(ve)
                            vop_(lambda e: e.tensor_tensor(rs_sc[:, 4:4 + n], rs_sc[:, 0:n],
                                                           rs_a[:, x0:x1_], ALU.mult))
                            DR(ve)
                            vop_(lambda e: e.tensor_scalar(rs_sc[:, 8:8 + n], rs_sc[:, 4:4 + n],
                                                           -0.5, 1.5, ALU.mult, ALU.add))
                            DR(ve)
                            vop_(lambda e: e.tensor_tensor(rs_sc[:, 12:12 + n], rs_b[:, y0:y1_],
                                                           rs_sc[:, 8:8 + n], ALU.mult))
                            DR(ve)
                            vop_(lambda e: e.tensor_copy(rs_b[:, y0:y1_], rs_sc[:, 12:12 + n]))
                            DR(ve)

                    newton4((0, 4), (0, 4))
                    W(ve, s_ld2)
                    for hh in range(4):
                        vop_(lambda e, hh=hh: e.tensor_scalar(
                            orms[:, hh * DV:(hh + 1) * DV], comb_b[:, hh * DV:(hh + 1) * DV],
                            rs_b[:, hh:hh + 1], None, ALU.mult))
                    W(sc, s_v)
                    W(sc, s_t)
                    for kk in range(8):
                        dmat(ofm[:, kk * 128:(kk + 1) * 128], orms[:, kk * 128:(kk + 1) * 128])
                    W(te, s_ld2)
                    for half in range(2):
                        pp = pA if half == 0 else pB
                        for kk in range(8):
                            I(te, lambda e, pp=pp, half=half, kk=kk: e.matmul(
                                pp[:, 0:512], ofm[:, kk * 128:(kk + 1) * 128],
                                wo_sb[:, kk * 1024 + half * 512: kk * 1024 + half * 512 + 512],
                                start=(kk == 0), stop=(kk == 7)), (s_t, 1))
                    W(sy, s_v)
                    ld(xt_f[:], x_own[XHALO + ot0 + ti * 128: XHALO + ot0 + (ti + 1) * 128, :])
                    W(ve, s_t)
                    W(ve, s_ld)
                    vop_(lambda e: e.tensor_tensor(y_t[:, 0:512], pA[:, 0:512], xt_f[:, 0:512], ALU.add))
                    vop_(lambda e: e.tensor_tensor(y_t[:, 512:1024], pB[:, 0:512],
                                                   xt_f[:, 512:1024], ALU.add))
                    W(te, s_v)
                    W(ve, s_a)
                    DR(ve)
                    vop_(lambda e: e.tensor_reduce(ln_sc[:, 0:1], y_t[:, 0:D], AX.X, ALU.add))
                    vop_(lambda e: e.tensor_tensor(tstage[:, 0:GRP], y_t[:, 0:GRP], y_t[:, 0:GRP], ALU.mult))
                    vop_(lambda e: e.tensor_tensor(tstage[:, GRP:2 * GRP], y_t[:, GRP:D],
                                                   y_t[:, GRP:D], ALU.mult))
                    DR(ve)
                    vop_(lambda e: e.tensor_reduce(ln_sc[:, 1:2], tstage[:, 0:2 * GRP], AX.X, ALU.add))
                    DR(ve)
                    vop_(lambda e: e.tensor_scalar(ln_sc[:, 2:3], ln_sc[:, 0:1], 1.0 / D, None, ALU.mult))
                    DR(ve)
                    vop_(lambda e: e.tensor_tensor(ln_sc[:, 3:4], ln_sc[:, 2:3], ln_sc[:, 2:3], ALU.mult))
                    vop_(lambda e: e.tensor_scalar(ln_sc[:, 4:5], ln_sc[:, 1:2], 1.0 / D, None, ALU.mult))
                    DR(ve)
                    vop_(lambda e: e.tensor_tensor(ln_sc[:, 5:6], ln_sc[:, 4:5], ln_sc[:, 3:4], ALU.subtract))
                    DR(ve)
                    vop_(lambda e: e.tensor_scalar(rs_a[:, 0:1], ln_sc[:, 5:6], 1.0, 1e-5, ALU.mult, ALU.add))
                    DR(ve)
                    newton4((0, 1), (0, 1), n=1)
                    vop_(lambda e: e.tensor_scalar(ln_sc[:, 10:11], ln_sc[:, 2:3], -1.0, None, ALU.mult))
                    DR(ve)
                    W(ve, s_ld)
                    vop_(lambda e: e.tensor_scalar(x1_t[:, 0:D], y_t[:, 0:D],
                                                   ln_sc[:, 10:11], rs_b[:, 0:1], ALU.add, ALU.mult))
                    DR(ve)
                    vop_(lambda e: e.tensor_reduce(ln_sc[:, 12:13], x1_t[:, 0:D], AX.X, ALU.max,
                                                   apply_absolute_value=True))
                    DR(ve)
                    vop_(lambda e: e.tensor_tensor(ln_sc[:, 13:14], amax_run[:, 0:1],
                                                   ln_sc[:, 12:13], ALU.max))
                    DR(ve)
                    vop_(lambda e: e.tensor_copy(amax_run[:, 0:1], ln_sc[:, 13:14]))
                    W(sy, s_v)
                    ld(x1_spill[ot0 + ti * 128: ot0 + (ti + 1) * 128, :], x1_t[:, 0:D])
                    if DEBUG and og == 0 and ti == 0:
                        I(sy, lambda e: e.dma_start(out=dbg["x1"], in_=x1_t[:]), (s_dbg, 16))
                    W(sy, s_dbg)
                    W(ve, s_dbg)
                    W(sc, s_dbg)
                    W(te, s_dbg)
                    W(gp, s_dbg)

            W(te, s_v)
            I(te, lambda e: e.transpose(pT_f[0:1, 0:128], amax_run[:, 0:1], ident_f[:]), (s_t, 1))
            W(ve, s_t)
            vop_(lambda e: e.tensor_reduce(sA[0:1, 40:41], pT_f[0:1, 0:128], AX.X, ALU.max))
            DR(ve)
            W(sy, s_v)
            if state["cur"] == sy:
                e = state["eng"]
                e.dma_start(out=state["rv"]["sc_wr"][:, 0:1], in_=sA[0:1, 40:41]).then_inc(s_ld, 16)
            state["cnt"][s_ld.name] = state["cnt"].get(s_ld.name, 0) + 16
            W(sy, s_ld)
            if state["cur"] == sy:
                e = state["eng"]
                e.dma_start(out=state["rv"]["flam_wr"][:, 0:1], in_=nonce_sb[0:1, 0:1]).then_inc(s_ld, 16)
                pollr, noncer = state["rv"]["pollr"], state["rv"]["noncer"]
                e.reg_mov(pollr, 1)
                wh = e.While(pollr)
                wh.__enter__()
                e.sem_clear(s_p)
                e.dma_start(out=flag_sb[0:1, 0:1], in_=state["rv"]["flam_rd"][:, 0:1]).then_inc(s_p, 16)
                e.wait_ge(s_p, 16)
                e.reg_load(pollr, flag_sb[0:1, 0:1])
                e.reg_alu(pollr, pollr, noncer, ALU.not_equal)
                wh.__exit__(None, None, None)
                e.dma_start(out=sA[0:1, 41:42], in_=state["rv"]["sc_rd"][:, 0:1]).then_inc(s_ld, 16)
            state["cnt"][s_ld.name] = state["cnt"].get(s_ld.name, 0) + 32
            W(ve, s_ld)
            vop_(lambda e: e.tensor_tensor(sA[0:1, 42:43], sA[0:1, 40:41], sA[0:1, 41:42], ALU.max))
            DR(ve)
            vop_(lambda e: e.tensor_scalar(sA[0:1, 43:44], sA[0:1, 42:43], 1e-12, None, ALU.max))
            DR(ve)
            vop_(lambda e: e.reciprocal(sA[0:1, 44:45], sA[0:1, 43:44]))
            DR(ve)
            vop_(lambda e: e.tensor_scalar(sA[0:1, 45:46], sA[0:1, 44:45], 448.0, 448.0,
                                           ALU.mult, ALU.min))
            DR(ve)
            vop_(lambda e: e.reciprocal(sA[0:1, 46:47], sA[0:1, 45:46]))
            DR(ve)
            vop_(lambda e: e.tensor_scalar(sA[0:1, 47:48], sA[0:1, 45:46], 0.5, None, ALU.mult))
            DR(ve)
            vop_(lambda e: e.tensor_tensor(sA[0:1, 48:49], sA[0:1, 46:47], sA[0:1, 7:8], ALU.mult))
            DR(ve)
            vop_(lambda e: e.tensor_scalar(sA[0:1, 49:50], sA[0:1, 48:49], 4.0, None, ALU.mult))
            DR(ve)
            W(sy, s_v)
            ld(loc_sc[0:1, 4:6], sA[0:1, 47:49][0:1, 0:2] if False else sA[0:1, 47:49])
            W(sy, s_ld)
            ld(loc_sc[0:1, 6:7], sA[0:1, 49:50])
            W(sy, s_ld)
            ld(sc_bc[:, 0:1], loc_sc[0:1, 4:5].broadcast_to((128, 1)))
            ld(sc_bc[:, 1:2], loc_sc[0:1, 6:7].broadcast_to((128, 1)))
            W(ve, s_ld)
            if DEBUG:
                W(sy, s_v)
                I(sy, lambda e: e.dma_start(out=dbg["amax"], in_=amax_run[:]), (s_dbg, 16))

            for kk in range(8):
                ld(wstage[:, 0:1024], wfT[kk * 128:(kk + 1) * 128, :])
                W(ve, s_ld)
                vop_(lambda e, kk=kk: e.tensor_scalar(
                    wf8_sb[:, kk * 1024:(kk + 1) * 1024], wstage[:, 0:1024],
                    ws2_bc[:, 0:1], None, ALU.mult))
                W(sy, s_v)
            for og in range(NOG_LIM):
                for ti in range(4):
                    r0 = og * GRP + ti * 128
                    W(sy, s_v)
                    ld(xt_f[:], x1_spill[r0:r0 + 128, :])
                    W(ve, s_ld)
                    W(ve, s_ld2)
                    vop_(lambda e: e.tensor_scalar(xs_b[:, 0:D], xt_f[:], sc_bc[:, 0:1], None, ALU.mult))
                    W(sc, s_v)
                    W(sc, s_t)
                    for kk in range(8):
                        dmat(xs_fmb[:, kk * 128:(kk + 1) * 128], xs_b[:, kk * 128:(kk + 1) * 128])
                    W(ve, s_ld2)
                    vop_(lambda e: e.tensor_copy(xs_fm8[:], xs_fmb[:]))
                    W(te, s_v)
                    for half in range(2):
                        pp = pA if half == 0 else pB
                        for kk in range(8):
                            I(te, lambda e, pp=pp, half=half, kk=kk: e.matmul(
                                pp[:, 0:512], xs_fm8[:, kk * 128:(kk + 1) * 128],
                                wf8_sb[:, kk * 1024 + half * 512: kk * 1024 + half * 512 + 512],
                                start=(kk == 0), stop=(kk == 7)), (s_t, 1))
                    W(ve, s_t)
                    W(ve, s_gd)
                    vop_(lambda e: e.scalar_tensor_tensor(ffn_t[:, 0:512], pA[:, 0:512],
                                                          sc_bc[:, 1:2], fbias_sb[:, 0:512],
                                                          ALU.mult, ALU.add))
                    vop_(lambda e: e.scalar_tensor_tensor(ffn_t[:, 512:1024], pB[:, 0:512],
                                                          sc_bc[:, 1:2], fbias_sb[:, 512:1024],
                                                          ALU.mult, ALU.add))
                    W(te, s_v)
                    DR(ve)
                    vop_(lambda e: e.tensor_tensor(y_t[:, 0:D], xt_f[:], ffn_t[:, 0:D], ALU.add))
                    DR(ve)
                    vop_(lambda e: e.tensor_reduce(ln_sc[:, 0:1], y_t[:, 0:D], AX.X, ALU.add))
                    vop_(lambda e: e.tensor_tensor(tstage[:, 0:GRP], y_t[:, 0:GRP], y_t[:, 0:GRP], ALU.mult))
                    vop_(lambda e: e.tensor_tensor(tstage[:, GRP:2 * GRP], y_t[:, GRP:D],
                                                   y_t[:, GRP:D], ALU.mult))
                    DR(ve)
                    vop_(lambda e: e.tensor_reduce(ln_sc[:, 1:2], tstage[:, 0:2 * GRP], AX.X, ALU.add))
                    DR(ve)
                    vop_(lambda e: e.tensor_scalar(ln_sc[:, 2:3], ln_sc[:, 0:1], 1.0 / D, None, ALU.mult))
                    DR(ve)
                    vop_(lambda e: e.tensor_tensor(ln_sc[:, 3:4], ln_sc[:, 2:3], ln_sc[:, 2:3], ALU.mult))
                    vop_(lambda e: e.tensor_scalar(ln_sc[:, 4:5], ln_sc[:, 1:2], 1.0 / D, None, ALU.mult))
                    DR(ve)
                    vop_(lambda e: e.tensor_tensor(ln_sc[:, 5:6], ln_sc[:, 4:5], ln_sc[:, 3:4], ALU.subtract))
                    DR(ve)
                    vop_(lambda e: e.tensor_scalar(rs_a[:, 0:1], ln_sc[:, 5:6], 1.0, 1e-5, ALU.mult, ALU.add))
                    DR(ve)
                    newton4((0, 1), (0, 1), n=1)
                    vop_(lambda e: e.tensor_scalar(ln_sc[:, 10:11], ln_sc[:, 2:3], -1.0, None, ALU.mult))
                    DR(ve)
                    W(ve, s_st)
                    vop_(lambda e: e.tensor_scalar(out_t[:, 0:D], y_t[:, 0:D],
                                                   ln_sc[:, 10:11], rs_b[:, 0:1], ALU.add, ALU.mult))
                    W(sy, s_v)
                    I(sy, lambda e, r0=r0: e.dma_start(out=out_d[r0:r0 + 128, :], in_=out_t[:, 0:D]),
                      (s_st, 16))

            W(sy, s_st)
            W(sy, s_ld)
            W(sy, s_ld2)


        @block.sync
        def _(sync):
            emit("sync", sync)

        @block.tensor
        def _(tensor):
            emit("tensor", tensor)

        @block.vector
        def _(vector):
            emit("vector", vector)

        @block.scalar
        def _(scalar):
            emit("scalar", scalar)

        @block.gpsimd
        def _(gpsimd):
            emit("gpsimd", gpsimd)

    return nc, out_d


_cache = {}


def kernel(**inputs):
    x = np.ascontiguousarray(inputs["L0_attn_hidden_states"], dtype=np.float32)
    Wq = np.asarray(inputs["L0_attn_Wq"], np.float32)
    Wk = np.asarray(inputs["L0_attn_Wk"], np.float32)
    Wv = np.asarray(inputs["L0_attn_Wv"], np.float32)
    Wb = np.asarray(inputs["L0_attn_Wb"], np.float32)
    qconv = np.asarray(inputs["L0_attn_qconv_w"], np.float32)
    kconv = np.asarray(inputs["L0_attn_kconv_w"], np.float32)
    vconv = np.asarray(inputs["L0_attn_vconv_w"], np.float32)
    fir_s = np.asarray(inputs["L0_attn_fir_short_w"], np.float32).reshape(D, 3)
    fir_l = np.asarray(inputs["L0_attn_fir_long_w"], np.float32).reshape(D, 63)
    g_w1 = np.asarray(inputs["L0_attn_gate_w1"], np.float32)
    g_b1 = np.asarray(inputs["L0_attn_gate_b1"], np.float32)
    g_w2 = np.asarray(inputs["L0_attn_gate_w2"], np.float32)
    log_temp = np.asarray(inputs["L0_attn_log_temp"], np.float32)
    base_bias = np.asarray(inputs["L0_attn_base_bias"], np.float32)
    floor_raw = np.asarray(inputs["L0_attn_floor_raw"], np.float32)
    onorm_w = np.asarray(inputs["L0_attn_onorm_w"], np.float32)
    Wo = np.asarray(inputs["L0_attn_Wo"], np.float32)
    Wf = np.asarray(inputs["L0_ffn_weight"], np.float32)
    fb = np.asarray(inputs["L0_ffn_bias"], np.float32)

    if "nc" not in _cache:
        _cache["nc"] = build_program()
    nc, out_d = _cache["nc"]

    def rbmajor(a):
        rb = a.shape[0] // 128
        return np.ascontiguousarray(a.reshape(rb, 128, -1).transpose(1, 0, 2).reshape(128, -1))

    nonce = np.frombuffer(os.urandom(8), np.int32).reshape(1, 2).copy()
    nonce[0, 0] |= 1

    in_maps = []
    for c in range(8):
        b, me = c // 2, c % 2
        hp0 = 512 * me
        xb_seq = x[b]
        xo = np.zeros((OWN + XHALO, D), np.float32)
        lo = me * OWN - XHALO
        src0 = max(lo, 0)
        xo[src0 - lo:] = xb_seq[src0: me * OWN + OWN]
        in_maps.append({
            "x_seq": xb_seq,
            "x_own": xo,
            "wqT": np.ascontiguousarray(Wq.T[:, hp0:hp0 + 512]),
            "wkT": np.ascontiguousarray(Wk.T[:, hp0:hp0 + 512]),
            "wbT": np.ascontiguousarray(Wb.T[:, 2 * me:2 * me + 2]),
            "wvT": np.ascontiguousarray(Wv.T),
            "wvsT": np.ascontiguousarray(Wv.T[:, hp0:hp0 + 512]),
            "w1T": np.ascontiguousarray(g_w1.T),
            "w2T": np.ascontiguousarray(g_w2.T),
            "woT": np.ascontiguousarray(Wo.T),
            "wfT": np.ascontiguousarray(Wf.T),
            "qcw": rbmajor(qconv[hp0:hp0 + 512]),
            "kcw": rbmajor(kconv[hp0:hp0 + 512]),
            "vcw": rbmajor(vconv),
            "vscw": rbmajor(vconv[hp0:hp0 + 512]),
            "firs": rbmajor(fir_s),
            "firl": rbmajor(fir_l),
            "gb1": rbmajor(g_b1.reshape(2048, 1)),
            "bb": np.ascontiguousarray(base_bias.reshape(16, 1)),
            "ltemp": log_temp.reshape(1, 4),
            "fraw": floor_raw.reshape(1, 16),
            "onw4": rbmajor(np.tile(onorm_w, H).reshape(D, 1)),
            "fbias": fb,
            "nonce": nonce,
            "meid": np.array([[me]], np.int32),
        })

    res = run_bass_kernel_spmd(nc, in_maps, core_ids=list(range(8)))
    out = np.empty((B, L, D), np.float32)
    for c in range(8):
        b, me = c // 2, c % 2
        out[b, me * OWN:(me + 1) * OWN] = res.results[c]["out"]
    return out
